# version 21
# baseline (speedup 1.0000x reference)
import contextlib
import numpy as np
import ml_dtypes
import concourse.bass as bass
import concourse.mybir as mybir
from concourse.bass_utils import run_bass_kernel_spmd

F32 = mybir.dt.float32
BF16 = mybir.dt.bfloat16
AF = mybir.ActivationFunctionType
ALU = mybir.AluOpType
AX = mybir.AxisListType

NCORES = 8
T_FULL = 16384
D = 1024
TC = 2048
WU = 2048
CP = 512
NB = (WU + TC) // CP
NTILE = (WU + TC) // 128
OWN_B0 = WU // CP

ENGINES = ("sync", "gpsimd", "scalar", "vector", "tensor")


class Prog:
    def __init__(self):
        self.ops = {e: [] for e in ENGINES}
        self.ctr = {}

    def emit(self, eng, fn, waits=(), incs=()):
        ws = []
        for (s, v) in waits:
            if not v:
                continue
            if s in ("qs", "qg", "qa"):
                v = self.ctr.get(s, 0)
            ws.append((s, v))
        ws = tuple(ws)
        self.ops[eng].append((ws, fn, tuple(incs)))
        for s, a in incs:
            self.ctr[s] = self.ctr.get(s, 0) + a
        return {s: self.ctr[s] for s, _ in incs}

    def val(self, s):
        return self.ctr.get(s, 0)


def build_nc(nbmax=NB, do_weights=True, parts=15, dbg_block=None, round_mode="ladder"):
    nc = bass.Bass("TRN2", target_bir_lowering=False)

    x_d = nc.dram_tensor("x", [WU + TC, D], F32, kind="ExternalInput").ap()
    ww_d = nc.dram_tensor("w_write", [D, D], F32, kind="ExternalInput").ap()
    wr_d = nc.dram_tensor("w_read", [D, D], F32, kind="ExternalInput").ap()
    wf_d = nc.dram_tensor("w_ffn", [D, D], F32, kind="ExternalInput").ap()
    h_d = nc.dram_tensor("hadamard", [D, D], BF16, kind="ExternalInput").ap()
    dsc_d = nc.dram_tensor("dscalars", [128, 2], F32, kind="ExternalInput").ap()
    idn_d = nc.dram_tensor("ident", [128, 128], BF16, kind="ExternalInput").ap()
    ones16_d = nc.dram_tensor("onesrow16", [1, 128], BF16, kind="ExternalInput").ap()
    bit_d = nc.dram_tensor("bitcols", [128, 7], F32, kind="ExternalInput").ap()
    fc_d = nc.dram_tensor("fconsts", [128, 8], F32, kind="ExternalInput").ap()
    cau_d = nc.dram_tensor("causalS", [128, 4, 512], BF16, kind="ExternalInput").ap()
    out_d = nc.dram_tensor("out", [TC, D], F32, kind="ExternalOutput").ap()
    if dbg_block is not None:
        dbg_vgs = nc.dram_tensor("dbg_vgs", [4, 128, D], F32, kind="ExternalOutput").ap()
        dbg_smt = nc.dram_tensor("dbg_smt", [4, 128, 512], F32, kind="ExternalOutput").ap()
        dbg_rds = nc.dram_tensor("dbg_rds", [8, 128, 512], F32, kind="ExternalOutput").ap()
        dbg_st = nc.dram_tensor("dbg_st", [8, 128, D], F32, kind="ExternalOutput").ap()
        dbg_xnT = nc.dram_tensor("dbg_xnT", [8, 128, 528], F32, kind="ExternalOutput").ap()
        dbg_xqT = nc.dram_tensor("dbg_xqT", [8, 128, 512], F32, kind="ExternalOutput").ap()
        dbg_cols = nc.dram_tensor("dbg_cols", [128, 28], F32, kind="ExternalOutput").ap()
        dbg_rows = nc.dram_tensor("dbg_rows", [4, D], F32, kind="ExternalOutput").ap()
        dbg_gcb = nc.dram_tensor("dbg_gcb", [128, 512], F32, kind="ExternalOutput").ap()
        dbg_msk = nc.dram_tensor("dbg_msk", [4, 128, 512], F32, kind="ExternalOutput").ap()

    C_ONE, C_NEPS, C_PEPS, C_85, C_I7, C_EPS7, C_1536, C_N8 = 0, 1, 2, 3, 4, 5, 6, 7

    ctx = contextlib.ExitStack()
    ent = ctx.enter_context
    with ctx:
        xln = ent(nc.sbuf_tensor("xln", [128, 2, D], F32))
        xcb = ent(nc.sbuf_tensor("xcb", [128, 2, D], F32))
        xntm = ent(nc.sbuf_tensor("xntm", [128, 6, D], BF16))
        xnT = ent(nc.sbuf_tensor("xnT", [128, 8, 528], BF16))
        vgs = ent(nc.sbuf_tensor("vgs", [128, 4, D], BF16))
        vg0 = ent(nc.sbuf_tensor("vg0", [1, D], BF16))
        wkr = ent(nc.sbuf_tensor("wkr", [1, D], BF16))
        srow = ent(nc.sbuf_tensor("srow", [1, 512], BF16))
        smt = ent(nc.sbuf_tensor("smt", [128, 4, 512], BF16))
        rds = ent(nc.sbuf_tensor("rds", [128, 8, 512], BF16))
        st32 = ent(nc.sbuf_tensor("st32", [128, 8, D], F32))
        st16 = ent(nc.sbuf_tensor("st16", [128, 3, D], BF16))
        xh16 = ent(nc.sbuf_tensor("xh16", [128, 4, D], BF16))
        u0q = ent(nc.sbuf_tensor("u0q", [128, D], mybir.dt.float16))
        uq = ent(nc.sbuf_tensor("uq", [128, D], mybir.dt.float16))
        cq = ent(nc.sbuf_tensor("cq", [128, D], BF16))
        xqtm = ent(nc.sbuf_tensor("xqtm", [128, 4, D], BF16))
        xqT = ent(nc.sbuf_tensor("xqT", [128, 8, 512], BF16))
        wwT = ent(nc.sbuf_tensor("wwT", [128, 8, D], BF16))
        hT = ent(nc.sbuf_tensor("hT", [128, 8, D], BF16))
        wqT = ent(nc.sbuf_tensor("wqT", [128, 8, D], BF16))
        warT = ent(nc.sbuf_tensor("warT", [128, 8, D], BF16))
        wst = ent(nc.sbuf_tensor("wst", [128, 1, D], F32))
        w16a = ent(nc.sbuf_tensor("w16a", [128, D], BF16))
        msk = ent(nc.sbuf_tensor("msk", [128, 4, 512], BF16))
        gcb = ent(nc.sbuf_tensor("gcb", [128, 512], BF16))
        cols = ent(nc.sbuf_tensor("cols", [128, 28], F32))
        stats = ent(nc.sbuf_tensor("stats", [128, 4, 8], F32))
        qst = ent(nc.sbuf_tensor("qst", [128, 2, 6], F32))
        idn = ent(nc.sbuf_tensor("idn", [128, 128], BF16))
        onesr16 = ent(nc.sbuf_tensor("onesr16", [1, 128], BF16))
        bitc = ent(nc.sbuf_tensor("bitc", [128, 7], F32))
        fcs = ent(nc.sbuf_tensor("fcs", [128, 8], F32))
        r16 = ent(nc.sbuf_tensor("r16", [128, 1], BF16))
        dscs = ent(nc.sbuf_tensor("dscs", [128, 2], F32))

        L_RPOS, L_RNEG, L_GC, L_ACOL, L_G512I = 0, 1, 2, 3, 4
        L_VGS0 = 5
        L_GB0 = 9
        L_G511, L_G384, L_GINV, L_G128I, L_GM2, L_E = 19, 20, 21, 22, 23, 24
        L_TMP = 25

        S_E, S_ALPHA = 0, 15

        pTa = ent(nc.psum_tensor("pTa", [128, 1024], BF16))
        pTb = ent(nc.psum_tensor("pTb", [128, 1024], BF16))
        pA = ent(nc.psum_tensor("pA", [128, 512], F32))
        pB = ent(nc.psum_tensor("pB", [128, 512], F32))
        pC = ent(nc.psum_tensor("pC", [128, 512], F32))
        pD = ent(nc.psum_tensor("pD", [128, 512], F32))
        pM = ent(nc.psum_tensor("pM", [128, 1024], F32))

        sems = {n: ent(nc.semaphore(n)) for n in ("qs", "qg", "qa", "te", "sc", "ve")}

        P = Prog()

        def dma(qeng, qsem, dst, src, waits=()):
            return P.emit(qeng, lambda e, d=dst, s=src: e.dma_start(out=d, in_=s),
                          waits=waits, incs=[(qsem, 16)])

        def op(eng, fn, waits=()):
            sem = {"scalar": "sc", "vector": "ve", "tensor": "te"}[eng]
            return P.emit(eng, fn, waits=waits, incs=[(sem, 1)])

        bank_free = {id(pA): None, id(pB): None, id(pC): None, id(pD): None}
        bname = {id(pA): pA, id(pB): pB, id(pC): pC, id(pD): pD}
        mm_par = [0]

        def take_bank():
            bank = (pA, pB)[mm_par[0] % 2]
            mm_par[0] += 1
            return bank

        d0 = dma("sync", "qs", dscs[:], dsc_d)
        dma("sync", "qs", idn[:], idn_d)
        dma("sync", "qs", onesr16[:], ones16_d)
        dma("sync", "qs", bitc[:], bit_d)
        dcn = dma("sync", "qs", fcs[:], fc_d)
        for j in range(8):
            dma("sync", "qs", hT[:, j, :], h_d[128 * j:128 * (j + 1), :])
        dmsk = dma("sync", "qs", msk[:], cau_d)

        op("scalar", lambda e: e.activation(cols[:, L_E:L_E + 1], dscs[:, 0:1], AF.Exp, scale=-1.0),
           waits=[("qs", d0["qs"])])
        op("scalar", lambda e: e.activation(cols[:, L_ACOL:L_ACOL + 1], dscs[:, 1:2], AF.Exp))
        sc_g0 = P.val("sc")
        ve_gam = P.val("ve")

        def _noop():
            pass

        def vcol(fn, waits=()):
            return op("vector", fn, waits=waits)

        vcol(lambda e: e.tensor_scalar(cols[:, L_E:L_E + 1], cols[:, L_E:L_E + 1],
                                       1.0, None, op0=ALU.add), waits=[("sc", sc_g0)])
        vcol(lambda e: e.reciprocal(cols[:, L_GB0:L_GB0 + 1], cols[:, L_E:L_E + 1]))
        for k in range(1, 10):
            vcol(lambda e, k=k: e.tensor_tensor(
                cols[:, L_GB0 + k:L_GB0 + k + 1], cols[:, L_GB0 + k - 1:L_GB0 + k],
                cols[:, L_GB0 + k - 1:L_GB0 + k], op=ALU.mult))
        vcol(lambda e: e.reciprocal(cols[:, L_G512I:L_G512I + 1], cols[:, L_GB0 + 9:L_GB0 + 10]))
        vcol(lambda e: e.tensor_copy(cols[:, L_GC:L_GC + 1], cols[:, L_GB0 + 9:L_GB0 + 10]))
        vcol(lambda e: e.reciprocal(cols[:, L_GINV:L_GINV + 1], cols[:, L_GB0:L_GB0 + 1]))
        vcol(lambda e: e.tensor_tensor(cols[:, L_G511:L_G511 + 1], cols[:, L_GB0 + 9:L_GB0 + 10],
                                       cols[:, L_GINV:L_GINV + 1], op=ALU.mult))
        vcol(lambda e: e.reciprocal(cols[:, L_G128I:L_G128I + 1], cols[:, L_GB0 + 7:L_GB0 + 8]))
        vcol(lambda e: e.reciprocal(cols[:, L_GM2:L_GM2 + 1], cols[:, L_GB0 + 1:L_GB0 + 2]))
        vcol(lambda e: e.tensor_tensor(cols[:, L_VGS0:L_VGS0 + 1], cols[:, L_GB0 + 9:L_GB0 + 10],
                                       cols[:, L_GM2:L_GM2 + 1], op=ALU.mult))
        for t in range(3):
            vcol(lambda e, t=t: e.tensor_tensor(
                cols[:, L_VGS0 + t + 1:L_VGS0 + t + 2], cols[:, L_VGS0 + t:L_VGS0 + t + 1],
                cols[:, L_G128I:L_G128I + 1], op=ALU.mult))
        vcol(lambda e: e.tensor_tensor(cols[:, L_G384:L_G384 + 1], cols[:, L_GB0 + 7:L_GB0 + 8],
                                       cols[:, L_GB0 + 8:L_GB0 + 9], op=ALU.mult))

        op("vector", lambda e: e.tensor_copy(cols[:, L_RPOS:L_RPOS + 1], fcs[:, C_ONE:C_ONE + 1]),
           waits=[("qs", dcn["qs"])])
        op("vector", lambda e: e.tensor_copy(cols[:, L_RNEG:L_RNEG + 1], fcs[:, C_ONE:C_ONE + 1]))

        def ramp_step(b, neg):
            dstc = L_RNEG if neg else L_RPOS
            if neg:
                op("vector", lambda e, b=b: e.reciprocal(cols[:, L_TMP:L_TMP + 1],
                                                         cols[:, L_GB0 + b:L_GB0 + b + 1]))
            else:
                op("vector", lambda e, b=b: e.tensor_copy(cols[:, L_TMP:L_TMP + 1],
                                                          cols[:, L_GB0 + b:L_GB0 + b + 1]))
            op("vector", lambda e: e.tensor_scalar(cols[:, L_TMP:L_TMP + 1],
                                                   cols[:, L_TMP:L_TMP + 1], -1.0, None, op0=ALU.add))
            op("vector", lambda e, b=b: e.scalar_tensor_tensor(
                cols[:, L_TMP + 1:L_TMP + 2], bitc[:, b:b + 1], cols[:, L_TMP:L_TMP + 1],
                fcs[:, C_ONE:C_ONE + 1], op0=ALU.mult, op1=ALU.add))
            op("vector", lambda e, d=dstc: e.tensor_tensor(
                cols[:, d:d + 1], cols[:, d:d + 1], cols[:, L_TMP + 1:L_TMP + 2], op=ALU.mult))

        for b in range(7):
            ramp_step(b, neg=False)
        for b in range(7):
            ramp_step(b, neg=True)
        for t in range(4):
            op("vector", lambda e, t=t: e.tensor_tensor(
                cols[:, L_VGS0 + t:L_VGS0 + t + 1], cols[:, L_VGS0 + t:L_VGS0 + t + 1],
                cols[:, L_RNEG:L_RNEG + 1], op=ALU.mult))
        ve_cols = P.val("ve")

        gsl = {1: L_GB0 + 7, 2: L_GB0 + 8, 3: L_G384}
        rc = op("vector", lambda e: e.tensor_copy(r16[:], cols[:, L_RPOS:L_RPOS + 1]),
                waits=[("ve", ve_cols)])
        last = None
        for t in range(4):
            bank = (pTa, pTb)[t % 2]
            w = [("ve", rc["ve"])]
            if last is not None:
                w.append(("sc", last["sc"]))
            m = op("tensor", lambda e, b=bank: e.matmul(
                b[0:1, 0:128], r16[:], idn[:], start=True, stop=True, is_transpose=True),
                waits=w + [("qs", P.val("qs"))])
            if t == 0:
                last = op("scalar", lambda e, b=bank: e.activation(
                    gcb[0:1, 0:128], b[0:1, 0:128], AF.Copy), waits=[("te", m["te"])])
            else:
                last = op("scalar", lambda e, b=bank, t=t: e.activation(
                    gcb[0:1, 128 * t:128 * (t + 1)], b[0:1, 0:128], AF.Copy,
                    scale=cols[0:1, gsl[t]:gsl[t] + 1]), waits=[("te", m["te"])])
        wpc = [("ve", P.val("ve"))]
        m = op("tensor", lambda e: e.matmul(pC[:], onesr16[:], gcb[0:1, :], start=True, stop=True),
               waits=[("sc", last["sc"])])
        r = op("vector", lambda e: e.tensor_copy(gcb[:], pC[:]), waits=[("te", m["te"])])
        ve_gcb = r["ve"]
        bank_free[id(pC)] = ("ve", ve_gcb)

        for k in range(4):
            op("vector", lambda e, k=k: e.tensor_scalar(
                msk[:, k, :], msk[:, k, :], cols[:, L_G512I:L_G512I + 1], None, op0=ALU.mult),
                waits=[("qs", dmsk["qs"]), ("ve", ve_cols)])
        ve_msk = P.val("ve")

        for j in range(8):
            op("vector", lambda e, j=j: e.memset(st32[:, j, :], 0.0))
        op("vector", lambda e: e.memset(xnT[:, :, 15:16], 0.0))
        op("vector", lambda e: e.memset(wkr[:], 0.0))
        op("vector", lambda e: e.memset(vg0[:], 0.0))
        op("vector", lambda e: e.memset(srow[:], 0.0))
        ve_init = P.val("ve")

        tpar = [0]
        ttile = [0]
        tT_free = [None, None]

        def transpose_tile(src_fn, evac_fn, extra_waits=()):
            marks = []
            for j in range(8):
                bank = (pTa, pTb)[tpar[0] % 2]
                off = 128 * (j // 2)
                w = list(extra_waits) if j == 0 else []
                if j <= 1 and tT_free[0]:
                    w.append(tT_free[0])
                m = op("tensor", lambda e, bk=bank, o=off, j=j: e.matmul(
                    bk[:, o:o + 128], src_fn(j), idn[:], start=True, stop=True,
                    is_transpose=True), waits=w)
                marks.append((bank, m))
                tpar[0] += 1
            first_a = marks[0][0] is pTa
            ba, bb = (pTa, pTb) if first_a else (pTb, pTa)
            mk = evac_fn(ba[:, 0:512], bb[:, 0:512], marks[-1][1]["te"])
            tT_free[0] = mk
            return mk

        def weight_rowtile(dst, r, src_ap, scale_col=None, quant=False):
            ws_slot = wslot[0] % 2
            wst_ap = wst[:, 0, :] if ws_slot == 0 else xcb[:, 1, :]
            wslot[0] += 1
            w = []
            if wst_free[ws_slot]:
                w.append(wst_free[ws_slot])
            dm = dma("sync", "qs", wst_ap, src_ap, waits=w)
            cw = [("qs", dm["qs"])]
            if w16a_free[0]:
                cw.append(w16a_free[0])
            if quant:
                v1 = op("vector", lambda e, s=wst_ap: e.tensor_reduce(
                    cols[:, L_TMP:L_TMP + 1], s, AX.X, ALU.add,
                    apply_absolute_value=True), waits=cw)
                s1 = op("scalar", lambda e: e.activation(
                    cols[:, L_TMP + 1:L_TMP + 2], cols[:, L_TMP:L_TMP + 1], AF.Relu,
                    scale=1.0 / D, bias=fcs[:, C_NEPS:C_NEPS + 1]), waits=[("ve", v1["ve"])])
                s2 = op("scalar", lambda e: e.activation(
                    cols[:, L_TMP + 1:L_TMP + 2], cols[:, L_TMP + 1:L_TMP + 2], AF.Identity,
                    bias=fcs[:, C_PEPS:C_PEPS + 1]))
                v2 = op("vector", lambda e: e.reciprocal(
                    cols[:, L_TMP + 2:L_TMP + 3], cols[:, L_TMP + 1:L_TMP + 2]),
                    waits=[("sc", s2["sc"])])
                s3 = op("scalar", lambda e, s=wst_ap: e.activation(
                    s, s, AF.Copy, scale=cols[:, L_TMP + 2:L_TMP + 3]),
                    waits=[("ve", v2["ve"])])
                v3 = op("vector", lambda e, s=wst_ap: e.tensor_scalar(
                    u0q[:], s, 0.5, None, op0=ALU.is_gt),
                    waits=[("sc", s3["sc"])] + ([uq_tern_free[0]] if uq_tern_free[0] else []))
                v4 = op("vector", lambda e, s=wst_ap: e.tensor_scalar(
                    uq[:], s, -0.5, None, op0=ALU.is_lt))
                wst_free[ws_slot] = ("ve", v4["ve"])
                v5 = op("vector", lambda e: e.tensor_tensor(cq[:], u0q[:], uq[:], op=ALU.subtract))
                c = op("scalar", lambda e: e.activation(
                    w16a[:], cq[:], AF.Copy, scale=cols[:, L_TMP + 1:L_TMP + 2]),
                    waits=[("ve", v5["ve"])] + cw[1:])
                uq_tern_free[0] = ("sc", c["sc"])
                uq_free[0] = ("sc", c["sc"])
            else:
                c = op("scalar", lambda e, s=wst_ap: e.activation(
                    w16a[:], s, AF.Copy), waits=cw)
                wst_free[ws_slot] = ("sc", c["sc"])

            def evac(ba, bb, te_done):
                if scale_col is not None:
                    op("scalar", lambda e, bk=ba: e.activation(
                        dst[:, 0:8:2, 128 * r:128 * (r + 1)], bk, AF.Copy,
                        scale=cols[:, scale_col:scale_col + 1]), waits=[("te", te_done)])
                    e2 = op("scalar", lambda e, bk=bb: e.activation(
                        dst[:, 1:8:2, 128 * r:128 * (r + 1)], bk, AF.Copy,
                        scale=cols[:, scale_col:scale_col + 1]), waits=[])
                    return ("sc", e2["sc"])
                op("vector", lambda e, bk=ba: e.tensor_copy(
                    dst[:, 0:8:2, 128 * r:128 * (r + 1)], bk), waits=[("te", te_done)])
                e2 = op("vector", lambda e, bk=bb: e.tensor_copy(
                    dst[:, 1:8:2, 128 * r:128 * (r + 1)], bk), waits=[])
                return ("ve", e2["ve"])

            mk = transpose_tile(lambda j: w16a[:, 128 * j:128 * (j + 1)], evac,
                                extra_waits=[("sc", c["sc"]), ("qs", dcn["qs"])])
            w16a_free[0] = mk
            return mk

        wst_free = [None, None]
        wslot = [0]
        w16a_free = [None]
        uq_tern_free = [None]

        wwT_mark = [None]

        ln_done = [None] * NTILE
        stats_done = [None] * NTILE
        xdma_done = [None] * NTILE
        tile_free = {}
        wkrow_done = [None] * (NB + 1)
        wkrow_done[0] = ("ve", ve_init)

        def prep_dma(g):
            b, t = divmod(g, 4)
            slot = g % 2
            w = []
            if g >= 2 and ln_done[g - 2] is not None:
                w.append(("sc", ln_done[g - 2]))
            dm = dma("sync", "qs", xln[:, slot, :], x_d[128 * g:128 * (g + 1), :], waits=w)
            xdma_done[g] = ("qs", dm["qs"])

        def prep_stats(g):
            b, t = divmod(g, 4)
            slot = g % 2
            r = g % 4
            v1 = op("vector", lambda e, s=slot, r=r: e.tensor_reduce(
                stats[:, r, 0:1], xln[:, s, :], AX.X, ALU.add), waits=[xdma_done[g]])
            sqw = [xdma_done[g]]
            if uq_free[0]:
                sqw.append(uq_free[0])
            s1 = op("vector", lambda e, s=slot, r=r: e.scalar_tensor_tensor(
                cq[:], xln[:, s, :], 1.0, xln[:, s, :], op0=ALU.mult, op1=ALU.mult,
                accum_out=stats[:, r, 1:2]), waits=sqw)
            op("vector", lambda e, r=r: e.tensor_scalar(
                stats[:, r, 2:3], stats[:, r, 0:1], 1.0 / D, None, op0=ALU.mult),
                waits=[("ve", v1["ve"])])
            op("vector", lambda e, r=r: e.tensor_scalar(
                stats[:, r, 3:4], stats[:, r, 1:2], 1.0 / D, None, op0=ALU.mult))
            op("vector", lambda e, r=r: e.tensor_tensor(
                stats[:, r, 4:5], stats[:, r, 2:3], stats[:, r, 2:3], op=ALU.mult))
            v3 = op("vector", lambda e, r=r: e.tensor_tensor(
                stats[:, r, 5:6], stats[:, r, 3:4], stats[:, r, 4:5], op=ALU.subtract))
            s2 = op("scalar", lambda e, r=r: e.activation(
                stats[:, r, 6:7], stats[:, r, 5:6], AF.Sqrt, bias=fcs[:, C_PEPS:C_PEPS + 1]),
                waits=[("ve", v3["ve"])])
            v4 = op("vector", lambda e, r=r: e.reciprocal(stats[:, r, 6:7], stats[:, r, 6:7]),
                    waits=[("sc", s2["sc"])])
            v5 = op("vector", lambda e, r=r: e.scalar_tensor_tensor(
                stats[:, r, 7:8], stats[:, r, 2:3], -1.0, stats[:, r, 6:7],
                op0=ALU.mult, op1=ALU.mult), waits=[("ve", v4["ve"])])
            stats_done[g] = v5["ve"]

        def prep_apply(g):
            b, t = divmod(g, 4)
            slot = g % 2
            r = g % 4
            lw = [("ve", stats_done[g])]
            for wv in tile_free.get(g - 6, []):
                lw.append(wv)
            s3 = op("scalar", lambda e, s=slot, r=r, b=b, t=t: e.activation(
                xntm[:, g % 6, :], xln[:, s, :], AF.Identity,
                scale=stats[:, r, 6:7], bias=stats[:, r, 7:8]), waits=lw)
            ln_done[g] = s3["sc"]

        tevac_done = [None] * NB
        xnT_rd_done = [None] * NB
        srow_mm_done = [None] * NB

        def trans_block(b):
            if b > 0:
                mw = [tevac_done[b - 1]]
                if srow_mm_done[b - 1]:
                    mw.append(("te", srow_mm_done[b - 1]))
                if xnT_rd_done[b - 1]:
                    mw.append(xnT_rd_done[b - 1])
                op("scalar", lambda e: e.activation(xnT[:, :, 15:16], xnT[:, :, 527:528], AF.Copy),
                   waits=mw)
            mg = ("sc", P.val("sc"))
            last = None
            for t in range(4):
                g = 4 * b + t
                tw = [("sc", ln_done[g])]
                if b > 0 and xnT_rd_done[b - 1]:
                    tw.append(xnT_rd_done[b - 1])
                c0 = 16 + 128 * t

                def evac(ba, bb, te_done, c0=c0):
                    op("vector", lambda e, bk=ba: e.tensor_copy(
                        xnT[:, 0:8:2, c0:c0 + 128], bk),
                        waits=[("te", te_done), mg])
                    e2 = op("vector", lambda e, bk=bb: e.tensor_copy(
                        xnT[:, 1:8:2, c0:c0 + 128], bk), waits=[])
                    return ("ve", e2["ve"])

                last = transpose_tile(
                    lambda j, g=g: xntm[:, g % 6, 128 * j:128 * (j + 1)], evac,
                    extra_waits=tw)
            tevac_done[b] = last

        vgs_done = [None] * NB
        sm_done = [None] * NB
        srow_ev = [None] * NB
        rd_done = [None] * NB
        stupd_done = [[None] * 8 for _ in range(NB)]
        m4_done = [None] * NB
        stupd_prev_half = [None, None]
        cmb_evac_free = [None, None]
        cmb_store = [None] * 16
        trans_cb = [None]
        mid_cb = [None]
        m4_cb = [None]
        pm_free = {0: None, 1: None}
        xh_done = [None] * 4
        xh_free = [None] * 4
        uq_free = [None]
        xq_free = [None] * 4
        xqT_done = [None] * 4
        qst_free = [None, None]

        def body(b):
            own = b >= OWN_B0
            for t in range(4):
                cnt = 127 if t == 3 else 128
                c0 = 17 + 128 * t
                for h in range(2):
                    bank = take_bank()
                    bw = [tevac_done[b]]
                    if b == 0 and t == 0 and h == 0 and wwT_mark[0]:
                        bw.append(wwT_mark[0])
                    if bank_free[id(bank)]:
                        bw.append(bank_free[id(bank)])
                    mk = None
                    for j in range(8):
                        mk = op("tensor", lambda e, bk=bank, j=j, c0=c0, cnt=cnt, h=h: e.matmul(
                            bk[0:cnt, :], xnT[:, j, c0:c0 + cnt], wwT[:, j, 512 * h:512 * (h + 1)],
                            start=(j == 0), stop=(j == 7)), waits=bw if j == 0 else [])
                    if (2 * t + h) % 2 == 0:
                        ev = op("scalar", lambda e, bk=bank, t=t, h=h, cnt=cnt: e.activation(
                            vgs[0:cnt, t, 512 * h:512 * (h + 1)], bk[0:cnt, :], AF.Copy,
                            scale=cols[0:cnt, L_VGS0 + t:L_VGS0 + t + 1]),
                            waits=[("te", mk["te"])])
                        bank_free[id(bank)] = ("sc", ev["sc"])
                    else:
                        ev = op("vector", lambda e, bk=bank, t=t, h=h, cnt=cnt: e.tensor_scalar(
                            vgs[0:cnt, t, 512 * h:512 * (h + 1)], bk[0:cnt, :],
                            cols[0:cnt, L_VGS0 + t:L_VGS0 + t + 1], None, op0=ALU.mult),
                            waits=[("te", mk["te"])])
                        bank_free[id(bank)] = ("ve", ev["ve"])
            for h in range(2):
                bank = take_bank()
                bw = [tevac_done[b]]
                if bank_free[id(bank)]:
                    bw.append(bank_free[id(bank)])
                mk = None
                for j in range(8):
                    mk = op("tensor", lambda e, bk=bank, j=j, h=h: e.matmul(
                        bk[0:1, :], xnT[:, j, 16:17], wwT[:, j, 512 * h:512 * (h + 1)],
                        start=(j == 0), stop=(j == 7)), waits=bw if j == 0 else [])
                ev = op("scalar", lambda e, bk=bank, h=h: e.activation(
                    vg0[0:1, 512 * h:512 * (h + 1)], bk[0:1, :], AF.Copy,
                    scale=cols[0:1, L_G511:L_G511 + 1]), waits=[("te", mk["te"])])
                bank_free[id(bank)] = ("sc", ev["sc"])
            vgs_done[b] = ("sc", P.val("sc"))
            if b < OWN_B0:
                xnT_rd_done[b] = ("te", P.val("te"))
                if mid_cb[0] is not None:
                    mid_cb[0]()
                    mid_cb[0] = None

            cast_sc = [None] * 8
            cast_last = [None] * 8
            early_cast = {}
            if own and (parts & 2):
                for j in range(3):
                    cw = []
                    if b > 0 and stupd_done[b - 1][j]:
                        cw.append(("ve", stupd_done[b - 1][j]))
                    if b > 0 and xnT_rd_done[b - 1]:
                        cw.append(xnT_rd_done[b - 1])
                    if j % 2 == 0:
                        ce = op("scalar", lambda e, j=j: e.activation(
                            st16[:, j, :], st32[:, j, :], AF.Copy), waits=cw)
                        early_cast[j] = ("sc", ce["sc"])
                    else:
                        ce = op("vector", lambda e, j=j: e.tensor_copy(
                            st16[:, j, :], st32[:, j, :]), waits=cw)
                        early_cast[j] = ("ve", ce["ve"])
            if own and not (parts & 1):
                own = False
            if own:
                for k in range(4):
                    cnt = 127 if k == 3 else 128
                    c0 = 16 + 128 * k
                    bank = take_bank()
                    bw = [tevac_done[b]]
                    if bank_free[id(bank)]:
                        bw.append(bank_free[id(bank)])
                    mk = None
                    for j in range(8):
                        mk = op("tensor", lambda e, bk=bank, j=j, c0=c0, cnt=cnt: e.matmul(
                            bk[0:cnt, :], xnT[:, j, c0:c0 + cnt], xnT[:, j, 16:528],
                            start=(j == 0), stop=(j == 7)), waits=bw if j == 0 else [])
                    ev = op("vector", lambda e, bk=bank, k=k, cnt=cnt: e.tensor_tensor(
                        smt[0:cnt, k, :], bk[0:cnt, :], msk[0:cnt, k, :], op=ALU.mult),
                        waits=[("te", mk["te"]), ("ve", ve_msk)])
                    bank_free[id(bank)] = ("ve", ev["ve"])
                sm_done[b] = ("ve", P.val("ve"))
                bank = take_bank()
                bw = [tevac_done[b]]
                if bank_free[id(bank)]:
                    bw.append(bank_free[id(bank)])
                mk = None
                for j in range(8):
                    mk = op("tensor", lambda e, bk=bank, j=j: e.matmul(
                        bk[0:1, :], xnT[:, j, 15:16], xnT[:, j, 16:528],
                        start=(j == 0), stop=(j == 7)), waits=bw if j == 0 else [])
                srow_mm_done[b] = mk["te"]
                ev = op("scalar", lambda e, bk=bank: e.activation(
                    srow[0:1, :], bk[0:1, :], AF.Copy, scale=cols[0:1, L_G512I:L_G512I + 1]),
                    waits=[("te", mk["te"])])
                z = op("vector", lambda e: e.memset(srow[0:1, 0:1], 0.0), waits=[("sc", ev["sc"])])
                bank_free[id(bank)] = ("ve", z["ve"])
                srow_ev[b] = ("ve", z["ve"])

                s4_marks = [None] * 4
                def emit_quant(t):
                    r = t % 2
                    qw = [("sc", xh_done[t])]
                    if qst_free[r]:
                        qw.append(qst_free[r])
                    v1 = op("vector", lambda e, t=t, r=r: e.tensor_reduce(
                        qst[:, r, 0:1], xh16[:, t, :], AX.X, ALU.max,
                        apply_absolute_value=True), waits=qw)
                    s1 = op("scalar", lambda e, r=r: e.activation(
                        qst[:, r, 1:2], qst[:, r, 0:1], AF.Relu,
                        bias=fcs[:, C_NEPS:C_NEPS + 1]), waits=[("ve", v1["ve"])])
                    s2 = op("scalar", lambda e, r=r: e.activation(
                        qst[:, r, 2:3], qst[:, r, 1:2], AF.Identity,
                        scale=fcs[:, C_I7:C_I7 + 1], bias=fcs[:, C_EPS7:C_EPS7 + 1]))
                    v2 = op("vector", lambda e, r=r: e.reciprocal(qst[:, r, 3:4], qst[:, r, 2:3]),
                            waits=[("sc", s2["sc"])])
                    op("vector", lambda e, r=r: e.tensor_scalar(
                        qst[:, r, 4:5], qst[:, r, 2:3],
                        -8.0 if round_mode == "ladder" else -1536.0, None, op0=ALU.mult))
                    uw = [("ve", v2["ve"])]
                    if uq_free[0]:
                        uw.append(uq_free[0])
                    if round_mode == "ladder":
                        s3 = op("scalar", lambda e, t=t, r=r: e.activation(
                            u0q[:], xh16[:, t, :], AF.Identity, scale=qst[:, r, 3:4],
                            bias=fcs[:, C_85:C_85 + 1]), waits=uw)
                        op("vector", lambda e: e.tensor_copy(uq[:], u0q[:]),
                           waits=[("sc", s3["sc"])])
                        for bit in (8.0, 4.0, 2.0, 1.0):
                            op("vector", lambda e, bit=bit: e.tensor_scalar(
                                cq[:], uq[:], bit, None, op0=ALU.is_ge))
                            op("vector", lambda e, bit=bit: e.scalar_tensor_tensor(
                                uq[:], cq[:], -bit, uq[:], op0=ALU.mult, op1=ALU.add))
                        v4 = op("vector", lambda e: e.tensor_tensor(cq[:], u0q[:], uq[:],
                                                                    op=ALU.subtract))
                        xw = [("ve", v4["ve"])]
                        if xq_free[t]:
                            xw.append(xq_free[t])
                        s4 = op("scalar", lambda e, t=t, r=r: e.activation(
                            xqtm[:, t, :], cq[:], AF.Identity, scale=qst[:, r, 2:3],
                            bias=qst[:, r, 4:5]), waits=xw)
                        xh_free[t] = ("ve", v4["ve"])
                        qst_free[r] = ("ve", v4["ve"])
                    else:
                        s3 = op("scalar", lambda e, t=t, r=r: e.activation(
                            u0q[:], xh16[:, t, :], AF.Identity, scale=qst[:, r, 3:4],
                            bias=fcs[:, C_1536:C_1536 + 1]), waits=uw)
                        xw = [("sc", s3["sc"])]
                        if xq_free[t]:
                            xw.append(xq_free[t])
                        s4 = op("scalar", lambda e, t=t, r=r: e.activation(
                            xqtm[:, t, :], u0q[:], AF.Identity, scale=qst[:, r, 2:3],
                            bias=qst[:, r, 4:5]), waits=xw)
                        xh_free[t] = ("sc", s3["sc"])
                        qst_free[r] = ("sc", s4["sc"])
                    uq_free[0] = ("sc", s4["sc"])
                    s4_marks[t] = s4


                m7rot = [0]
                for t in range(4 if (parts & 4) else 0):
                    c0 = 16 + 128 * t
                    for h in range(2):
                        rk_ = m7rot[0] % 4
                        m7rot[0] += 1
                        if rk_ < 2:
                            bank_ap = pM[:, 512 * rk_:512 * (rk_ + 1)]
                            fkey = ("pm", rk_)
                        else:
                            bank_ap = (pA, pB)[rk_ - 2][:]
                            fkey = ("b", id((pA, pB)[rk_ - 2]))
                        bw = []
                        fm = pm_free[fkey[1]] if fkey[0] == "pm" else bank_free[fkey[1]]
                        if fm:
                            bw.append(fm)
                        if xh_free[t]:
                            bw.append(xh_free[t])
                        mk = None
                        for j in range(8):
                            mk = op("tensor", lambda e, bk=bank_ap, j=j, c0=c0, h=h: e.matmul(
                                bk, xnT[:, j, c0:c0 + 128], hT[:, j, 512 * h:512 * (h + 1)],
                                start=(j == 0), stop=(j == 7)), waits=bw if j == 0 else [])
                        ev = op("scalar", lambda e, bk=bank_ap, t=t, h=h: e.activation(
                            xh16[:, t, 512 * h:512 * (h + 1)], bk, AF.Copy),
                            waits=[("te", mk["te"])])
                        if fkey[0] == "pm":
                            pm_free[fkey[1]] = ("sc", ev["sc"])
                        else:
                            bank_free[fkey[1]] = ("sc", ev["sc"])
                    xh_done[t] = P.val("sc")

                def emit_txq(t):
                    s4 = s4_marks[t]

                    def evq(ba, bb, te_done, t=t):
                        op("scalar", lambda e, bk=ba: e.activation(
                            xqT[:, 0:8:2, 128 * t:128 * (t + 1)], bk, AF.Copy),
                            waits=[("te", te_done)])
                        e2 = op("scalar", lambda e, bk=bb: e.activation(
                            xqT[:, 1:8:2, 128 * t:128 * (t + 1)], bk, AF.Copy),
                            waits=[])
                        return ("sc", e2["sc"])

                    mkq = transpose_tile(
                        lambda j, t=t: xqtm[:, t, 128 * j:128 * (j + 1)], evq,
                        extra_waits=[("sc", s4["sc"])])
                    xq_free[t] = mkq
                    xqT_done[t] = mkq

                banks4 = [pA, pB, pC, pD]
                do_m5 = bool(parts & 2)
                skip_m3 = bool(parts & 16)
                skip_bnd = bool(parts & 32)
                njm5 = 1 if (parts & 64) else 8
                nhm5 = 1 if (parts & 128) else 2
                last_cast_te = None
                for h in range(nhm5 if do_m5 else 0):
                    cast_te_hist = {}
                    cast_mark = {}
                    for j in range(njm5):
                        if h == 0 and j <= 2 and j in early_cast:
                            cast_mark[j] = early_cast[j]
                            cast_last[j] = early_cast[j]
                            cwl = None
                        else:
                            cwl = []
                            if b > 0 and stupd_done[b - 1][j]:
                                cwl.append(("ve", stupd_done[b - 1][j]))
                            if j >= 3 and (j - 3) in cast_te_hist:
                                cwl.append(("te", cast_te_hist[j - 3]))
                            elif j <= 2 and h == 1 and last_cast_te:
                                cwl.append(("te", last_cast_te))
                        if cwl is None:
                            pass
                        elif j % 2 == 0:
                            cst = op("scalar", lambda e, j=j: e.activation(
                                st16[:, j % 3, :], st32[:, j, :], AF.Copy), waits=cwl)
                            cast_mark[j] = ("sc", cst["sc"])
                            cast_sc[j] = cst["sc"]
                        else:
                            cst = op("vector", lambda e, j=j: e.tensor_copy(
                                st16[:, j % 3, :], st32[:, j, :]), waits=cwl)
                            cast_mark[j] = ("ve", cst["ve"])
                            cast_sc[j] = None
                        mk = None
                        for ii in range(4):
                            i = 4 * h + ii
                            bank = banks4[ii]
                            bw = [cast_mark[j]]
                            if j == 0:
                                bw.append(tevac_done[b])
                                if bank_free[id(bank)]:
                                    bw.append(bank_free[id(bank)])
                            stp = (j == njm5 - 1) and skip_m3 and skip_bnd
                            mk = op("tensor", lambda e, bk=bank, j=j, i=i, stp=stp: e.matmul(
                                bk[:], st16[:, j % 3, 128 * i:128 * (i + 1)],
                                xnT[:, j, 16:528], start=(j == 0), stop=stp),
                                waits=bw)
                        cast_te_hist[j] = mk["te"]
                        cast_last[j] = cast_mark[j]
                    last_cast_te = cast_te_hist[7]
                    if not skip_m3:
                        for k in range(4):
                            cnt = 127 if k == 3 else 128
                            for ii in range(4):
                                i = 4 * h + ii
                                bank = banks4[ii]
                                bw = []
                                if k == 0 and ii == 0:
                                    bw = [sm_done[b], vgs_done[b]]
                                stp = (k == 3) and skip_bnd
                                op("tensor", lambda e, bk=bank, k=k, i=i, cnt=cnt, stp=stp: e.matmul(
                                    bk[:], vgs[0:cnt, k, 128 * i:128 * (i + 1)],
                                    smt[0:cnt, k, :], start=False, stop=stp), waits=bw)
                    for ii in range(4):
                        i = 4 * h + ii
                        bank = banks4[ii]
                        if not skip_bnd:
                            bw = [srow_ev[b], vgs_done[b]] if ii == 0 else []
                            mk = op("tensor", lambda e, bk=bank, i=i: e.matmul(
                                bk[:], vg0[0:1, 128 * i:128 * (i + 1)], srow[0:1, :],
                                start=False, stop=True), waits=bw)
                        tew = P.val("te")
                        ev = op("vector", lambda e, bk=bank, i=i: e.tensor_tensor(
                            rds[:, i, :], bk[:], gcb[:], op=ALU.mult),
                            waits=[("te", tew), ("ve", ve_gcb)])
                        bank_free[id(bank)] = ("ve", ev["ve"])
                    if (parts & 4) and h == 0:
                        emit_quant(0)
                    if (parts & 4) and h == 1:
                        emit_quant(1)
                    if h == 0 and mid_cb[0] is not None:
                        mid_cb[0]()
                        mid_cb[0] = None
                rd_done[b] = ("ve", P.val("ve")) if do_m5 else None
                xnT_rd_done[b] = ("te", P.val("te"))

            abanks = [(pM[:, 0:512], ("pm", 0)), (pM[:, 512:1024], ("pm", 1)),
                      (pA, ("b", id(pA))), (pB, ("b", id(pB))),
                      (pC, ("b", id(pC))), (pD, ("b", id(pD)))]
            bseq = [0, 1, 2, 3, 4, 5, 0, 1, 2, 3, 4, 5, 2, 3, 4, 5]
            aupd = [None] * 6
            gk = 0
            for e_ch in range(8):
                for h in range(2):
                    bank_ap, bkey = abanks[bseq[gk]]
                    bw = [vgs_done[b], wkrow_done[b]]
                    if aupd[bseq[gk]]:
                        bw.append(("ve", aupd[bseq[gk]]))
                    elif bkey[0] == "pm":
                        if pm_free[bkey[1]]:
                            bw.append(pm_free[bkey[1]])
                    else:
                        if bank_free[bkey[1]]:
                            bw.append(bank_free[bkey[1]])
                    ap = bank_ap if bkey[0] == "pm" else bank_ap[:]
                    mk = None
                    for k in range(4):
                        cnt = 127 if k == 3 else 128
                        mk = op("tensor", lambda e, k=k, e_ch=e_ch, h=h, cnt=cnt, b=b, ap=ap: e.matmul(
                            ap, xntm[0:cnt, (4 * b + k) % 6, 128 * e_ch:128 * (e_ch + 1)],
                            vgs[0:cnt, k, 512 * h:512 * (h + 1)],
                            start=(k == 0), stop=False), waits=bw if k == 0 else [])
                    mk = op("tensor", lambda e, e_ch=e_ch, h=h, ap=ap: e.matmul(
                        ap, wkr[0:1, 128 * e_ch:128 * (e_ch + 1)],
                        vg0[0:1, 512 * h:512 * (h + 1)], start=False, stop=True))
                    uw = [("te", mk["te"])]
                    if own and cast_last[e_ch]:
                        uw.append(cast_last[e_ch])
                    u = op("vector", lambda e, e_ch=e_ch, h=h, ap=ap: e.scalar_tensor_tensor(
                        st32[:, e_ch, 512 * h:512 * (h + 1)], st32[:, e_ch, 512 * h:512 * (h + 1)],
                        cols[:, L_GC:L_GC + 1], ap,
                        op0=ALU.mult, op1=ALU.add), waits=uw)
                    aupd[bseq[gk]] = u["ve"]
                    if bkey[0] == "pm":
                        pm_free[bkey[1]] = ("ve", u["ve"])
                    else:
                        bank_free[bkey[1]] = ("ve", u["ve"])
                    gk += 1
                stupd_done[b][e_ch] = P.val("ve")
                if own and (parts & 4) and e_ch in (1, 3, 5, 7):
                    emit_txq((e_ch - 1) // 2)
                if own and (parts & 4) and e_ch == 1:
                    emit_quant(2)
                if own and (parts & 4) and e_ch == 5:
                    emit_quant(3)
            m4_done[b] = P.val("te")
            if not own:
                for k in range(4):
                    tile_free.setdefault(4 * b + k, []).append(("te", m4_done[b]))

            if own:
                ob = b - OWN_B0
                for t in range(4 if (parts & 8) else 0):
                    g_own = 4 * ob + t
                    lw = []
                    if g_own >= 2 and cmb_store[g_own - 2] is not None:
                        lw.append(("qg", cmb_store[g_own - 2]))
                    dm = dma("gpsimd", "qg", xcb[:, t % 2, :],
                             x_d[WU + 512 * ob + 128 * t: WU + 512 * ob + 128 * t + 128, :],
                             waits=lw)
                    last_ev = None
                    for h in range(2):
                        bw = [rd_done[b], xqT_done[t]]
                        if pm_free[h]:
                            bw.append(pm_free[h])
                        if cmb_evac_free[h]:
                            bw.append(("ve", cmb_evac_free[h]))
                        mk = None
                        for i in range(8):
                            mk = op("tensor", lambda e, i=i, t=t, h=h: e.matmul(
                                pM[:, 512 * h:512 * (h + 1)], rds[:, i, 128 * t:128 * (t + 1)],
                                warT[:, i, 512 * h:512 * (h + 1)],
                                start=(i == 0), stop=False), waits=bw if i == 0 else [])
                        for j in range(8):
                            mk = op("tensor", lambda e, j=j, t=t, h=h: e.matmul(
                                pM[:, 512 * h:512 * (h + 1)], xqT[:, j, 128 * t:128 * (t + 1)],
                                wqT[:, j, 512 * h:512 * (h + 1)], start=False, stop=(j == 7)))
                        ev1 = op("vector", lambda e, t=t, h=h: e.tensor_tensor(
                            xcb[:, t % 2, 512 * h:512 * (h + 1)],
                            xcb[:, t % 2, 512 * h:512 * (h + 1)],
                            pM[:, 512 * h:512 * (h + 1)], op=ALU.add),
                            waits=[("te", mk["te"]), ("qg", dm["qg"])])
                        ev2 = op("vector", lambda e, t=t, h=h, b=b: e.scalar_tensor_tensor(
                            xcb[:, t % 2, 512 * h:512 * (h + 1)],
                            xntm[:, (4 * b + t) % 6, 512 * h:512 * (h + 1)], 1.0,
                            xcb[:, t % 2, 512 * h:512 * (h + 1)],
                            op0=ALU.mult, op1=ALU.add), waits=[("ve", ev1["ve"])])
                        cmb_evac_free[h] = ev2["ve"]
                        pm_free[h] = ("ve", ev1["ve"])
                        last_ev = ev2
                    tile_free.setdefault(4 * b + t, []).append(("ve", last_ev["ve"]))
                    st = dma("gpsimd", "qg",
                             out_d[512 * ob + 128 * t: 512 * ob + 128 * t + 128, :],
                             xcb[:, t % 2, :], waits=[("ve", last_ev["ve"])])
                    cmb_store[g_own] = st["qg"]

        if nbmax > 0:
            for g in range(4):
                prep_dma(g)
                prep_stats(g)
                prep_apply(g)
            trans_block(0)
        if do_weights or nbmax > 0:
            for r in range(8):
                wwT_mark[0] = weight_rowtile(wwT, r, ww_d[128 * r:128 * (r + 1), :])
        for b in range(min(nbmax, NB)):
            if b == 1 and do_weights:
                for r in range(8):
                    weight_rowtile(wqT, r, wf_d[128 * r:128 * (r + 1), :], quant=True)
            if b == 2 and do_weights:
                for r in range(8):
                    weight_rowtile(warT, r, wr_d[128 * r:128 * (r + 1), :], scale_col=L_ACOL)
            if b + 1 < min(nbmax, NB):
                for g in range(4 * (b + 1), 4 * (b + 1) + 2):
                    prep_dma(g)
                    prep_stats(g)

                def _mid(bb=b + 1, cur=b):
                    prep_apply(4 * bb)
                    prep_apply(4 * bb + 1)
                    prep_dma(4 * bb + 2)
                    prep_dma(4 * bb + 3)
                    if cur < OWN_B0:
                        prep_stats(4 * bb + 2)
                        prep_stats(4 * bb + 3)
                mid_cb[0] = _mid

                def _cb(bb=b + 1, cur=b):
                    if cur >= OWN_B0:
                        prep_stats(4 * bb + 2)
                        prep_stats(4 * bb + 3)
                    prep_apply(4 * bb + 2)
                    prep_apply(4 * bb + 3)
                    trans_block(bb)
                trans_cb[0] = _cb
            body(b)
            if trans_cb[0] is not None:
                trans_cb[0]()
                trans_cb[0] = None
            if b + 1 < min(nbmax, NB):
                w = [("sc", ln_done[4 * b + 3]), ("te", m4_done[b])]
                dmw = dma("gpsimd", "qg", wkr[0:1, :], xntm[127:128, (4 * b + 3) % 6, :], waits=w)
                wkrow_done[b + 1] = ("qg", dmw["qg"])
                tile_free.setdefault(4 * b + 3, []).append(("qg", dmw["qg"]))

        if dbg_block is not None:
            dstg = xcb[:, 0, :]
            dbw = [("te", P.val("te")), ("sc", P.val("sc")), ("ve", P.val("ve")),
                   ("qg", P.val("qg"))]

            def dump16(dst, src_ap):
                w = dbw + ([("qs", P.val("qs"))] if False else [])
                n = src_ap.shape[-1]
                p = src_ap.shape[0]
                c = op("scalar", lambda e, d=src_ap, n=n, p=p: e.activation(dstg[0:p, 0:n], d, AF.Copy),
                       waits=dbw + [("qs", P.val("qs"))])
                dma("sync", "qs", dst, dstg[0:p, 0:n], waits=[("sc", c["sc"])])

            for t in range(4):
                dump16(dbg_vgs[t], vgs[:, t, :])
                dump16(dbg_smt[t], smt[:, t, :])
                dump16(dbg_msk[t], msk[:, t, :])
            for i in range(8):
                dump16(dbg_rds[i], rds[:, i, :])
                dump16(dbg_xnT[i], xnT[:, i, :])
                dump16(dbg_xqT[i], xqT[:, i, :])
                dma("sync", "qs", dbg_st[i], st32[:, i, :], waits=dbw)
            dma("sync", "qs", dbg_cols[:], cols[:], waits=dbw)
            dump16(dbg_gcb[:], gcb[:])
            dump16(dbg_rows[0:1], vg0[0:1, :])
            dump16(dbg_rows[1:2], wkr[0:1, :])
            dump16(dbg_rows[2:3, 0:512], srow[0:1, :])
            P.emit("sync", lambda e: e.wait_ge(sems["qs"], P.val("qs")), waits=(), incs=())
        P.emit("gpsimd", lambda e: e.wait_ge(sems["qg"], P.val("qg")), waits=(), incs=())

        globals()["_LAST_PROG"] = P
        with nc.Block() as block:
            selfsem = {"vector": "ve", "scalar": "sc"}

            def mkrun(name):
                def run(eng_obj):
                    own = selfsem.get(name)
                    prior = 0
                    for waits, fn, incs in P.ops[name]:
                        if own and prior:
                            eng_obj.wait_ge(sems[own], prior)
                        for s, v in waits:
                            eng_obj.wait_ge(sems[s], v)
                        inst = fn(eng_obj)
                        for s, a in incs:
                            inst.then_inc(sems[s], a)
                        if own:
                            prior += sum(a for s, a in incs if s == own)
                return run

            block.sync(mkrun("sync"))
            block.gpsimd(mkrun("gpsimd"))
            block.scalar(mkrun("scalar"))
            block.vector(mkrun("vector"))
            block.tensor(mkrun("tensor"))
    return nc


def _host_consts():
    bf = ml_dtypes.bfloat16
    ident = np.eye(128, dtype=bf)
    onesr = np.ones((1, 128), dtype=np.float32)
    onesr16 = np.ones((1, 128), dtype=bf)
    r = np.arange(128)
    bitc = np.stack([((r >> b) & 1).astype(np.float32) for b in range(7)], axis=1)
    bitc = np.ascontiguousarray(bitc)
    fc = np.zeros((128, 8), dtype=np.float32)
    fc[:, 0] = 1.0
    fc[:, 1] = -1e-5
    fc[:, 2] = 1e-5
    fc[:, 3] = 8.5
    fc[:, 4] = 1.0 / 7.0
    fc[:, 5] = 1e-5 / 7.0
    fc[:, 6] = 1536.0
    fc[:, 7] = -8.0
    kk = np.arange(512)[:, None]
    cc = np.arange(512)[None, :]
    cauS = (cc >= kk + 2).astype(bf)
    cauS = np.ascontiguousarray(cauS.reshape(4, 128, 512).transpose(1, 0, 2))
    h = np.array([[1.0]], dtype=np.float64)
    while h.shape[0] < D:
        h = np.block([[h, h], [h, -h]])
    had = np.ascontiguousarray((h / np.sqrt(D)).astype(np.float32).astype(bf))
    return ident, onesr, onesr16, bitc, fc, cauS, had


_NC_CACHE = [None]


def kernel(L0_attn_out, L0_attn_W_write, L0_attn_W_read, L0_attn_decay, L0_attn_log_alpha,
           L0_ffn_weight):
    x = np.ascontiguousarray(np.asarray(L0_attn_out, dtype=np.float32).reshape(T_FULL, D))
    ww = np.ascontiguousarray(np.asarray(L0_attn_W_write, dtype=np.float32))
    wr = np.ascontiguousarray(np.asarray(L0_attn_W_read, dtype=np.float32))
    wf = np.ascontiguousarray(np.asarray(L0_ffn_weight, dtype=np.float32))
    dsc = np.tile(np.array([[float(np.asarray(L0_attn_decay)),
                             float(np.asarray(L0_attn_log_alpha))]], dtype=np.float32), (128, 1))
    ident, onesr, onesr16, bitc, fc, cauS, had = _host_consts()

    if _NC_CACHE[0] is None:
        _NC_CACHE[0] = build_nc()
    nc = _NC_CACHE[0]

    xpad = np.concatenate([np.zeros((WU, D), dtype=np.float32), x], axis=0)
    in_maps = []
    for c in range(NCORES):
        s = c * TC
        in_maps.append({
            "x": np.ascontiguousarray(xpad[s: s + WU + TC]),
            "w_write": ww, "w_read": wr, "w_ffn": wf,
            "hadamard": had, "dscalars": dsc, "ident": ident,
            "onesrow": onesr, "onesrow16": onesr16, "bitcols": bitc, "fconsts": fc, "causalS": cauS,
        })
    res = run_bass_kernel_spmd(nc, in_maps, core_ids=list(range(NCORES)))
    out = np.concatenate([np.asarray(r["out"], dtype=np.float32) for r in res.results], axis=0)
    return out.reshape(1, T_FULL, D)


# version 22
# speedup vs baseline: 1.2063x; 1.2063x over previous
import contextlib
import numpy as np
import ml_dtypes
import concourse.bass as bass
import concourse.mybir as mybir
from concourse.bass_utils import run_bass_kernel_spmd

F32 = mybir.dt.float32
BF16 = mybir.dt.bfloat16
AF = mybir.ActivationFunctionType
ALU = mybir.AluOpType
AX = mybir.AxisListType

NCORES = 8
T_FULL = 16384
D = 1024
TC = 2048
WU = 2048
CP = 512
NB = (WU + TC) // CP
NTILE = (WU + TC) // 128
OWN_B0 = WU // CP

ENGINES = ("sync", "gpsimd", "scalar", "vector", "tensor")


class Prog:
    def __init__(self):
        self.ops = {e: [] for e in ENGINES}
        self.ctr = {}

    def emit(self, eng, fn, waits=(), incs=()):
        ws = []
        for (s, v) in waits:
            if not v:
                continue
            if s in ("qs", "qg", "qa"):
                v = self.ctr.get(s, 0)
            ws.append((s, v))
        ws = tuple(ws)
        self.ops[eng].append((ws, fn, tuple(incs)))
        for s, a in incs:
            self.ctr[s] = self.ctr.get(s, 0) + a
        return {s: self.ctr[s] for s, _ in incs}

    def val(self, s):
        return self.ctr.get(s, 0)


def build_nc(nbmax=NB, do_weights=True, parts=15, dbg_block=None, round_mode="ladder"):
    nc = bass.Bass("TRN2", target_bir_lowering=False)

    x_d = nc.dram_tensor("x", [WU + TC, D], F32, kind="ExternalInput").ap()
    ww_d = nc.dram_tensor("w_write", [D, D], F32, kind="ExternalInput").ap()
    wr_d = nc.dram_tensor("w_read", [D, D], F32, kind="ExternalInput").ap()
    wf_d = nc.dram_tensor("w_ffn", [D, D], F32, kind="ExternalInput").ap()
    h_d = nc.dram_tensor("hadamard", [D, D], BF16, kind="ExternalInput").ap()
    dsc_d = nc.dram_tensor("dscalars", [128, 2], F32, kind="ExternalInput").ap()
    idn_d = nc.dram_tensor("ident", [128, 128], BF16, kind="ExternalInput").ap()
    ones16_d = nc.dram_tensor("onesrow16", [1, 128], BF16, kind="ExternalInput").ap()
    bit_d = nc.dram_tensor("bitcols", [128, 8], F32, kind="ExternalInput").ap()
    fc_d = nc.dram_tensor("fconsts", [128, 8], F32, kind="ExternalInput").ap()
    cau_d = nc.dram_tensor("causalS", [128, 4, 512], BF16, kind="ExternalInput").ap()
    out_d = nc.dram_tensor("out", [TC, D], F32, kind="ExternalOutput").ap()
    if dbg_block is not None:
        dbg_vgs = nc.dram_tensor("dbg_vgs", [4, 128, D], F32, kind="ExternalOutput").ap()
        dbg_smt = nc.dram_tensor("dbg_smt", [4, 128, 512], F32, kind="ExternalOutput").ap()
        dbg_rds = nc.dram_tensor("dbg_rds", [8, 128, 512], F32, kind="ExternalOutput").ap()
        dbg_st = nc.dram_tensor("dbg_st", [8, 128, D], F32, kind="ExternalOutput").ap()
        dbg_xnT = nc.dram_tensor("dbg_xnT", [8, 128, 528], F32, kind="ExternalOutput").ap()
        dbg_xqT = nc.dram_tensor("dbg_xqT", [8, 128, 512], F32, kind="ExternalOutput").ap()
        dbg_cols = nc.dram_tensor("dbg_cols", [128, 28], F32, kind="ExternalOutput").ap()
        dbg_rows = nc.dram_tensor("dbg_rows", [4, D], F32, kind="ExternalOutput").ap()
        dbg_gcb = nc.dram_tensor("dbg_gcb", [128, 512], F32, kind="ExternalOutput").ap()
        dbg_msk = nc.dram_tensor("dbg_msk", [4, 128, 512], F32, kind="ExternalOutput").ap()

    C_ONE, C_NEPS, C_PEPS, C_85, C_I7, C_EPS7, C_1536, C_N8 = 0, 1, 2, 3, 4, 5, 6, 7

    ctx = contextlib.ExitStack()
    ent = ctx.enter_context
    with ctx:
        xln = ent(nc.sbuf_tensor("xln", [128, 2, D], F32))
        xcb = ent(nc.sbuf_tensor("xcb", [128, 2, D], F32))
        xntm = ent(nc.sbuf_tensor("xntm", [128, 6, D], BF16))
        xnT = ent(nc.sbuf_tensor("xnT", [128, 8, 528], BF16))
        vgs = ent(nc.sbuf_tensor("vgs", [128, 4, D], BF16))
        vg0 = ent(nc.sbuf_tensor("vg0", [1, D], BF16))
        wkr = ent(nc.sbuf_tensor("wkr", [1, D], BF16))
        srow = ent(nc.sbuf_tensor("srow", [1, 512], BF16))
        smt = ent(nc.sbuf_tensor("smt", [128, 4, 512], BF16))
        rds = ent(nc.sbuf_tensor("rds", [128, 8, 512], BF16))
        st32 = ent(nc.sbuf_tensor("st32", [128, 8, D], F32))
        st16 = ent(nc.sbuf_tensor("st16", [128, 3, D], BF16))
        xh16 = ent(nc.sbuf_tensor("xh16", [128, 4, D], BF16))
        u0q = ent(nc.sbuf_tensor("u0q", [128, D], mybir.dt.float16))
        uq = ent(nc.sbuf_tensor("uq", [128, D], mybir.dt.float16))
        cq = ent(nc.sbuf_tensor("cq", [128, D], BF16))
        xqtm = ent(nc.sbuf_tensor("xqtm", [128, 4, D], BF16))
        xqT = ent(nc.sbuf_tensor("xqT", [128, 8, 512], BF16))
        wwT = ent(nc.sbuf_tensor("wwT", [128, 8, D], BF16))
        hT = ent(nc.sbuf_tensor("hT", [128, 8, D], BF16))
        wqT = ent(nc.sbuf_tensor("wqT", [128, 8, D], BF16))
        warT = ent(nc.sbuf_tensor("warT", [128, 8, D], BF16))
        wst = ent(nc.sbuf_tensor("wst", [128, 1, D], F32))
        w16a = ent(nc.sbuf_tensor("w16a", [128, D], BF16))
        msk = ent(nc.sbuf_tensor("msk", [128, 4, 512], BF16))
        gcb = ent(nc.sbuf_tensor("gcb", [128, 512], BF16))
        cols = ent(nc.sbuf_tensor("cols", [128, 28], F32))
        stats = ent(nc.sbuf_tensor("stats", [128, 4, 8], F32))
        qst = ent(nc.sbuf_tensor("qst", [128, 2, 6], F32))
        idn = ent(nc.sbuf_tensor("idn", [128, 128], BF16))
        onesr16 = ent(nc.sbuf_tensor("onesr16", [1, 128], BF16))
        bitc = ent(nc.sbuf_tensor("bitc", [128, 8], F32))
        fcs = ent(nc.sbuf_tensor("fcs", [128, 8], F32))
        r16 = ent(nc.sbuf_tensor("r16", [128, 1], BF16))
        dscs = ent(nc.sbuf_tensor("dscs", [128, 2], F32))

        L_RPOS, L_RNEG, L_GC, L_ACOL, L_G512I = 0, 1, 2, 3, 4
        L_VGS0 = 5
        L_GB0 = 9
        L_G511, L_G384, L_GINV, L_G128I, L_GM2, L_E = 19, 20, 21, 22, 23, 24
        L_TMP = 25

        S_E, S_ALPHA = 0, 15

        pTa = ent(nc.psum_tensor("pTa", [128, 1024], BF16))
        pTb = ent(nc.psum_tensor("pTb", [128, 1024], BF16))
        pA = ent(nc.psum_tensor("pA", [128, 512], F32))
        pB = ent(nc.psum_tensor("pB", [128, 512], F32))
        pC = ent(nc.psum_tensor("pC", [128, 512], F32))
        pD = ent(nc.psum_tensor("pD", [128, 512], F32))
        pM = ent(nc.psum_tensor("pM", [128, 1024], F32))

        sems = {n: ent(nc.semaphore(n)) for n in ("qs", "qg", "qa", "te", "sc", "ve")}

        P = Prog()

        def dma(qeng, qsem, dst, src, waits=()):
            return P.emit(qeng, lambda e, d=dst, s=src: e.dma_start(out=d, in_=s),
                          waits=waits, incs=[(qsem, 16)])

        def op(eng, fn, waits=()):
            sem = {"scalar": "sc", "vector": "ve", "tensor": "te"}[eng]
            return P.emit(eng, fn, waits=waits, incs=[(sem, 1)])

        bank_free = {id(pA): None, id(pB): None, id(pC): None, id(pD): None}
        bname = {id(pA): pA, id(pB): pB, id(pC): pC, id(pD): pD}
        mm_par = [0]

        def take_bank():
            bank = (pA, pB)[mm_par[0] % 2]
            mm_par[0] += 1
            return bank

        d0 = dma("sync", "qs", dscs[:], dsc_d)
        dma("sync", "qs", idn[:], idn_d)
        dma("sync", "qs", onesr16[:], ones16_d)
        dma("sync", "qs", bitc[:], bit_d)
        dcn = dma("sync", "qs", fcs[:], fc_d)
        for j in range(8):
            dma("sync", "qs", hT[:, j, :], h_d[128 * j:128 * (j + 1), :])
        dmsk = dma("sync", "qs", msk[:], cau_d)

        op("scalar", lambda e: e.activation(cols[:, L_E:L_E + 1], dscs[:, 0:1], AF.Exp, scale=-1.0),
           waits=[("qs", d0["qs"])])
        op("scalar", lambda e: e.activation(cols[:, L_ACOL:L_ACOL + 1], dscs[:, 1:2], AF.Exp))
        sc_g0 = P.val("sc")
        ve_gam = P.val("ve")

        def _noop():
            pass

        def vcol(fn, waits=()):
            return op("vector", fn, waits=waits)

        vcol(lambda e: e.tensor_scalar(cols[:, L_E:L_E + 1], cols[:, L_E:L_E + 1],
                                       1.0, None, op0=ALU.add), waits=[("sc", sc_g0)])
        g1 = vcol(lambda e: e.reciprocal(cols[:, L_GB0:L_GB0 + 1], cols[:, L_E:L_E + 1]))
        for k in range(1, 10):
            vcol(lambda e, k=k: e.tensor_tensor(
                cols[:, L_GB0 + k:L_GB0 + k + 1], cols[:, L_GB0 + k - 1:L_GB0 + k],
                cols[:, L_GB0 + k - 1:L_GB0 + k], op=ALU.mult))
        vcol(lambda e: e.reciprocal(cols[:, L_G512I:L_G512I + 1], cols[:, L_GB0 + 9:L_GB0 + 10]))
        vcol(lambda e: e.tensor_copy(cols[:, L_GC:L_GC + 1], cols[:, L_GB0 + 9:L_GB0 + 10]))
        vcol(lambda e: e.reciprocal(cols[:, L_GINV:L_GINV + 1], cols[:, L_GB0:L_GB0 + 1]))
        vcol(lambda e: e.tensor_tensor(cols[:, L_G511:L_G511 + 1], cols[:, L_GB0 + 9:L_GB0 + 10],
                                       cols[:, L_GINV:L_GINV + 1], op=ALU.mult))
        vcol(lambda e: e.reciprocal(cols[:, L_G128I:L_G128I + 1], cols[:, L_GB0 + 7:L_GB0 + 8]))
        vcol(lambda e: e.reciprocal(cols[:, L_GM2:L_GM2 + 1], cols[:, L_GB0 + 1:L_GB0 + 2]))
        vcol(lambda e: e.tensor_tensor(cols[:, L_VGS0:L_VGS0 + 1], cols[:, L_GB0 + 9:L_GB0 + 10],
                                       cols[:, L_GM2:L_GM2 + 1], op=ALU.mult))
        for t in range(3):
            vcol(lambda e, t=t: e.tensor_tensor(
                cols[:, L_VGS0 + t + 1:L_VGS0 + t + 2], cols[:, L_VGS0 + t:L_VGS0 + t + 1],
                cols[:, L_G128I:L_G128I + 1], op=ALU.mult))
        vcol(lambda e: e.tensor_tensor(cols[:, L_G384:L_G384 + 1], cols[:, L_GB0 + 7:L_GB0 + 8],
                                       cols[:, L_GB0 + 8:L_GB0 + 9], op=ALU.mult))

        lgs = op("scalar", lambda e: e.activation(
            cols[:, L_TMP:L_TMP + 1], cols[:, L_GB0:L_GB0 + 1], AF.Ln),
            waits=[("ve", g1["ve"])])
        op("scalar", lambda e: e.activation(
            cols[:, L_TMP + 1:L_TMP + 2], cols[:, L_TMP:L_TMP + 1], AF.Copy, scale=-1.0))
        op("scalar", lambda e: e.activation(
            cols[:, L_RPOS:L_RPOS + 1], bitc[:, 7:8], AF.Exp,
            scale=cols[:, L_TMP:L_TMP + 1]), waits=[("qs", dcn["qs"])])
        sr = op("scalar", lambda e: e.activation(
            cols[:, L_RNEG:L_RNEG + 1], bitc[:, 7:8], AF.Exp,
            scale=cols[:, L_TMP + 1:L_TMP + 2]))
        sc_ramps = sr["sc"]
        for t in range(4):
            op("vector", lambda e, t=t: e.tensor_tensor(
                cols[:, L_VGS0 + t:L_VGS0 + t + 1], cols[:, L_VGS0 + t:L_VGS0 + t + 1],
                cols[:, L_RNEG:L_RNEG + 1], op=ALU.mult),
                waits=[("sc", sc_ramps)] if t == 0 else [])
        ve_cols = P.val("ve")

        ve_gcb = None
        ve_msk = None

        def emit_gcb():
            nonlocal ve_gcb, ve_msk
            gsl = {1: L_GB0 + 7, 2: L_GB0 + 8, 3: L_G384}
            rc = op("vector", lambda e: e.tensor_copy(r16[:], cols[:, L_RPOS:L_RPOS + 1]),
                    waits=[("ve", ve_cols), ("sc", sc_ramps)])
            last = None
            for t in range(4):
                bank = (pTa, pTb)[t % 2]
                w = [("ve", rc["ve"])]
                if t < 2 and tT_free[0]:
                    w.append(tT_free[0])
                if last is not None:
                    w.append(("sc", last["sc"]))
                m = op("tensor", lambda e, b=bank: e.matmul(
                    b[0:1, 0:128], r16[:], idn[:], start=True, stop=True, is_transpose=True),
                    waits=w + [("qs", P.val("qs"))])
                if t == 0:
                    last = op("scalar", lambda e, b=bank: e.activation(
                        gcb[0:1, 0:128], b[0:1, 0:128], AF.Copy), waits=[("te", m["te"])])
                else:
                    last = op("scalar", lambda e, b=bank, t=t: e.activation(
                        gcb[0:1, 128 * t:128 * (t + 1)], b[0:1, 0:128], AF.Copy,
                        scale=cols[0:1, gsl[t]:gsl[t] + 1]), waits=[("te", m["te"])])
            tT_free[0] = ("sc", last["sc"])
            m = op("tensor", lambda e: e.matmul(pC[:], onesr16[:], gcb[0:1, :], start=True, stop=True),
                   waits=[("sc", last["sc"])])
            r = op("vector", lambda e: e.tensor_copy(gcb[:], pC[:]), waits=[("te", m["te"])])
            ve_gcb = r["ve"]
            bank_free[id(pC)] = ("ve", ve_gcb)

            for k in range(4):
                op("vector", lambda e, k=k: e.tensor_scalar(
                    msk[:, k, :], msk[:, k, :], cols[:, L_G512I:L_G512I + 1], None, op0=ALU.mult),
                    waits=[("qs", dmsk["qs"]), ("ve", ve_cols)])
            ve_msk = P.val("ve")

        for j in range(8):
            op("vector", lambda e, j=j: e.memset(st32[:, j, :], 0.0))
        op("vector", lambda e: e.memset(xnT[:, :, 15:16], 0.0))
        op("vector", lambda e: e.memset(wkr[:], 0.0))
        op("vector", lambda e: e.memset(vg0[:], 0.0))
        op("vector", lambda e: e.memset(srow[:], 0.0))
        ve_init = P.val("ve")

        tpar = [0]
        ttile = [0]
        tT_free = [None, None]

        def transpose_tile(src_fn, evac_fn, extra_waits=()):
            marks = []
            for j in range(8):
                bank = (pTa, pTb)[tpar[0] % 2]
                off = 128 * (j // 2)
                w = list(extra_waits) if j == 0 else []
                if j <= 1 and tT_free[0]:
                    w.append(tT_free[0])
                m = op("tensor", lambda e, bk=bank, o=off, j=j: e.matmul(
                    bk[:, o:o + 128], src_fn(j), idn[:], start=True, stop=True,
                    is_transpose=True), waits=w)
                marks.append((bank, m))
                tpar[0] += 1
            first_a = marks[0][0] is pTa
            ba, bb = (pTa, pTb) if first_a else (pTb, pTa)
            mk = evac_fn(ba[:, 0:512], bb[:, 0:512], marks[-1][1]["te"])
            tT_free[0] = mk
            return mk

        def weight_rowtile(dst, r, src_ap, scale_col=None, quant=False):
            ws_slot = wslot[0] % 2
            wst_ap = wst[:, 0, :] if ws_slot == 0 else xcb[:, 1, :]
            wslot[0] += 1
            w = []
            if wst_free[ws_slot]:
                w.append(wst_free[ws_slot])
            dm = dma("sync", "qs", wst_ap, src_ap, waits=w)
            cw = [("qs", dm["qs"])]
            if w16a_free[0]:
                cw.append(w16a_free[0])
            if quant:
                v1 = op("vector", lambda e, s=wst_ap: e.tensor_reduce(
                    cols[:, L_TMP:L_TMP + 1], s, AX.X, ALU.add,
                    apply_absolute_value=True), waits=cw + [("sc", sc_ramps)])
                s1 = op("scalar", lambda e: e.activation(
                    cols[:, L_TMP + 1:L_TMP + 2], cols[:, L_TMP:L_TMP + 1], AF.Relu,
                    scale=1.0 / D, bias=fcs[:, C_NEPS:C_NEPS + 1]), waits=[("ve", v1["ve"])])
                s2 = op("scalar", lambda e: e.activation(
                    cols[:, L_TMP + 1:L_TMP + 2], cols[:, L_TMP + 1:L_TMP + 2], AF.Identity,
                    bias=fcs[:, C_PEPS:C_PEPS + 1]))
                v2 = op("vector", lambda e: e.reciprocal(
                    cols[:, L_TMP + 2:L_TMP + 3], cols[:, L_TMP + 1:L_TMP + 2]),
                    waits=[("sc", s2["sc"])])
                s3 = op("scalar", lambda e, s=wst_ap: e.activation(
                    s, s, AF.Copy, scale=cols[:, L_TMP + 2:L_TMP + 3]),
                    waits=[("ve", v2["ve"])])
                v3 = op("vector", lambda e, s=wst_ap: e.tensor_scalar(
                    u0q[:], s, 0.5, None, op0=ALU.is_gt),
                    waits=[("sc", s3["sc"])] + ([uq_tern_free[0]] if uq_tern_free[0] else []))
                v4 = op("vector", lambda e, s=wst_ap: e.tensor_scalar(
                    uq[:], s, -0.5, None, op0=ALU.is_lt))
                wst_free[ws_slot] = ("ve", v4["ve"])
                v5 = op("vector", lambda e: e.tensor_tensor(cq[:], u0q[:], uq[:], op=ALU.subtract))
                c = op("scalar", lambda e: e.activation(
                    w16a[:], cq[:], AF.Copy, scale=cols[:, L_TMP + 1:L_TMP + 2]),
                    waits=[("ve", v5["ve"])] + cw[1:])
                uq_tern_free[0] = ("sc", c["sc"])
                uq_free[0] = ("sc", c["sc"])
            else:
                c = op("scalar", lambda e, s=wst_ap: e.activation(
                    w16a[:], s, AF.Copy), waits=cw)
                wst_free[ws_slot] = ("sc", c["sc"])

            def evac(ba, bb, te_done):
                if scale_col is not None:
                    op("scalar", lambda e, bk=ba: e.activation(
                        dst[:, 0:8:2, 128 * r:128 * (r + 1)], bk, AF.Copy,
                        scale=cols[:, scale_col:scale_col + 1]), waits=[("te", te_done)])
                    e2 = op("scalar", lambda e, bk=bb: e.activation(
                        dst[:, 1:8:2, 128 * r:128 * (r + 1)], bk, AF.Copy,
                        scale=cols[:, scale_col:scale_col + 1]), waits=[])
                    return ("sc", e2["sc"])
                op("vector", lambda e, bk=ba: e.tensor_copy(
                    dst[:, 0:8:2, 128 * r:128 * (r + 1)], bk), waits=[("te", te_done)])
                e2 = op("vector", lambda e, bk=bb: e.tensor_copy(
                    dst[:, 1:8:2, 128 * r:128 * (r + 1)], bk), waits=[])
                return ("ve", e2["ve"])

            mk = transpose_tile(lambda j: w16a[:, 128 * j:128 * (j + 1)], evac,
                                extra_waits=[("sc", c["sc"]), ("qs", dcn["qs"])])
            w16a_free[0] = mk
            return mk

        wst_free = [None, None]
        wslot = [0]
        w16a_free = [None]
        uq_tern_free = [None]

        wwT_mark = [None]

        ln_done = [None] * NTILE
        stats_done = [None] * NTILE
        xdma_done = [None] * NTILE
        tile_free = {}
        wkrow_done = [None] * (NB + 1)
        wkrow_done[0] = ("ve", ve_init)

        def prep_dma(g):
            b, t = divmod(g, 4)
            slot = g % 2
            w = []
            if g >= 2 and ln_done[g - 2] is not None:
                w.append(("sc", ln_done[g - 2]))
            dm = dma("sync", "qs", xln[:, slot, :], x_d[128 * g:128 * (g + 1), :], waits=w)
            xdma_done[g] = ("qs", dm["qs"])

        def prep_stats(g):
            b, t = divmod(g, 4)
            slot = g % 2
            r = g % 4
            v1 = op("vector", lambda e, s=slot, r=r: e.tensor_reduce(
                stats[:, r, 0:1], xln[:, s, :], AX.X, ALU.add), waits=[xdma_done[g]])
            sqw = [xdma_done[g]]
            if uq_free[0]:
                sqw.append(uq_free[0])
            s1 = op("vector", lambda e, s=slot, r=r: e.scalar_tensor_tensor(
                cq[:], xln[:, s, :], 1.0, xln[:, s, :], op0=ALU.mult, op1=ALU.mult,
                accum_out=stats[:, r, 1:2]), waits=sqw)
            op("vector", lambda e, r=r: e.tensor_scalar(
                stats[:, r, 2:3], stats[:, r, 0:1], 1.0 / D, None, op0=ALU.mult),
                waits=[("ve", v1["ve"])])
            op("vector", lambda e, r=r: e.tensor_scalar(
                stats[:, r, 3:4], stats[:, r, 1:2], 1.0 / D, None, op0=ALU.mult))
            op("vector", lambda e, r=r: e.tensor_tensor(
                stats[:, r, 4:5], stats[:, r, 2:3], stats[:, r, 2:3], op=ALU.mult))
            v3 = op("vector", lambda e, r=r: e.tensor_tensor(
                stats[:, r, 5:6], stats[:, r, 3:4], stats[:, r, 4:5], op=ALU.subtract))
            s2 = op("scalar", lambda e, r=r: e.activation(
                stats[:, r, 6:7], stats[:, r, 5:6], AF.Sqrt, bias=fcs[:, C_PEPS:C_PEPS + 1]),
                waits=[("ve", v3["ve"])])
            v4 = op("vector", lambda e, r=r: e.reciprocal(stats[:, r, 6:7], stats[:, r, 6:7]),
                    waits=[("sc", s2["sc"])])
            v5 = op("vector", lambda e, r=r: e.scalar_tensor_tensor(
                stats[:, r, 7:8], stats[:, r, 2:3], -1.0, stats[:, r, 6:7],
                op0=ALU.mult, op1=ALU.mult), waits=[("ve", v4["ve"])])
            stats_done[g] = v5["ve"]

        def prep_apply(g):
            b, t = divmod(g, 4)
            slot = g % 2
            r = g % 4
            lw = [("ve", stats_done[g])]
            for wv in tile_free.get(g - 6, []):
                lw.append(wv)
            s3 = op("scalar", lambda e, s=slot, r=r, b=b, t=t: e.activation(
                xntm[:, g % 6, :], xln[:, s, :], AF.Identity,
                scale=stats[:, r, 6:7], bias=stats[:, r, 7:8]), waits=lw)
            ln_done[g] = s3["sc"]

        tevac_done = [None] * NB
        xnT_rd_done = [None] * NB
        srow_mm_done = [None] * NB

        def trans_block(b):
            if b > 0:
                mw = [tevac_done[b - 1]]
                if srow_mm_done[b - 1]:
                    mw.append(("te", srow_mm_done[b - 1]))
                if xnT_rd_done[b - 1]:
                    mw.append(xnT_rd_done[b - 1])
                op("scalar", lambda e: e.activation(xnT[:, :, 15:16], xnT[:, :, 527:528], AF.Copy),
                   waits=mw)
            mg = ("sc", P.val("sc"))
            last = None
            for t in range(4):
                g = 4 * b + t
                tw = [("sc", ln_done[g])]
                if b > 0 and xnT_rd_done[b - 1]:
                    tw.append(xnT_rd_done[b - 1])
                c0 = 16 + 128 * t

                def evac(ba, bb, te_done, c0=c0):
                    op("vector", lambda e, bk=ba: e.tensor_copy(
                        xnT[:, 0:8:2, c0:c0 + 128], bk),
                        waits=[("te", te_done), mg])
                    e2 = op("vector", lambda e, bk=bb: e.tensor_copy(
                        xnT[:, 1:8:2, c0:c0 + 128], bk), waits=[])
                    return ("ve", e2["ve"])

                last = transpose_tile(
                    lambda j, g=g: xntm[:, g % 6, 128 * j:128 * (j + 1)], evac,
                    extra_waits=tw)
            tevac_done[b] = last

        vgs_done = [None] * NB
        sm_done = [None] * NB
        srow_ev = [None] * NB
        rd_done = [None] * NB
        stupd_done = [[None] * 8 for _ in range(NB)]
        m4_done = [None] * NB
        stupd_prev_half = [None, None]
        cmb_evac_free = [None, None]
        cmb_store = [None] * 16
        trans_cb = [None]
        mid_cb = [None]
        m4_cb = [None]
        pm_free = {0: None, 1: None}
        xh_done = [None] * 4
        xh_free = [None] * 4
        uq_free = [None]
        xq_free = [None] * 4
        xqT_done = [None] * 4
        qst_free = [None, None]

        def body(b):
            own = b >= OWN_B0
            for t in range(4):
                cnt = 127 if t == 3 else 128
                c0 = 17 + 128 * t
                for h in range(2):
                    bank = take_bank()
                    bw = [tevac_done[b]]
                    if b == 0 and t == 0 and h == 0 and wwT_mark[0]:
                        bw.append(wwT_mark[0])
                    if bank_free[id(bank)]:
                        bw.append(bank_free[id(bank)])
                    mk = None
                    for j in range(8):
                        mk = op("tensor", lambda e, bk=bank, j=j, c0=c0, cnt=cnt, h=h: e.matmul(
                            bk[0:cnt, :], xnT[:, j, c0:c0 + cnt], wwT[:, j, 512 * h:512 * (h + 1)],
                            start=(j == 0), stop=(j == 7)), waits=bw if j == 0 else [])
                    if (2 * t + h) % 2 == 0:
                        ev = op("scalar", lambda e, bk=bank, t=t, h=h, cnt=cnt: e.activation(
                            vgs[0:cnt, t, 512 * h:512 * (h + 1)], bk[0:cnt, :], AF.Copy,
                            scale=cols[0:cnt, L_VGS0 + t:L_VGS0 + t + 1]),
                            waits=[("te", mk["te"])])
                        bank_free[id(bank)] = ("sc", ev["sc"])
                    else:
                        ev = op("vector", lambda e, bk=bank, t=t, h=h, cnt=cnt: e.tensor_scalar(
                            vgs[0:cnt, t, 512 * h:512 * (h + 1)], bk[0:cnt, :],
                            cols[0:cnt, L_VGS0 + t:L_VGS0 + t + 1], None, op0=ALU.mult),
                            waits=[("te", mk["te"])])
                        bank_free[id(bank)] = ("ve", ev["ve"])
            for h in range(2):
                bank = take_bank()
                bw = [tevac_done[b]]
                if bank_free[id(bank)]:
                    bw.append(bank_free[id(bank)])
                mk = None
                for j in range(8):
                    mk = op("tensor", lambda e, bk=bank, j=j, h=h: e.matmul(
                        bk[0:1, :], xnT[:, j, 16:17], wwT[:, j, 512 * h:512 * (h + 1)],
                        start=(j == 0), stop=(j == 7)), waits=bw if j == 0 else [])
                ev = op("scalar", lambda e, bk=bank, h=h: e.activation(
                    vg0[0:1, 512 * h:512 * (h + 1)], bk[0:1, :], AF.Copy,
                    scale=cols[0:1, L_G511:L_G511 + 1]), waits=[("te", mk["te"])])
                bank_free[id(bank)] = ("sc", ev["sc"])
            vgs_done[b] = ("sc", P.val("sc"))
            if b < OWN_B0:
                xnT_rd_done[b] = ("te", P.val("te"))
                if mid_cb[0] is not None:
                    mid_cb[0]()
                    mid_cb[0] = None

            cast_sc = [None] * 8
            cast_last = [None] * 8
            early_cast = {}
            if own and (parts & 2):
                for j in range(3):
                    cw = []
                    if b > 0 and stupd_done[b - 1][j]:
                        cw.append(("ve", stupd_done[b - 1][j]))
                    if b > 0 and xnT_rd_done[b - 1]:
                        cw.append(xnT_rd_done[b - 1])
                    if j % 2 == 0:
                        ce = op("scalar", lambda e, j=j: e.activation(
                            st16[:, j, :], st32[:, j, :], AF.Copy), waits=cw)
                        early_cast[j] = ("sc", ce["sc"])
                    else:
                        ce = op("vector", lambda e, j=j: e.tensor_copy(
                            st16[:, j, :], st32[:, j, :]), waits=cw)
                        early_cast[j] = ("ve", ce["ve"])
            if own and not (parts & 1):
                own = False
            if own:
                for k in range(4):
                    cnt = 127 if k == 3 else 128
                    c0 = 16 + 128 * k
                    bank = take_bank()
                    bw = [tevac_done[b]]
                    if bank_free[id(bank)]:
                        bw.append(bank_free[id(bank)])
                    mk = None
                    for j in range(8):
                        mk = op("tensor", lambda e, bk=bank, j=j, c0=c0, cnt=cnt: e.matmul(
                            bk[0:cnt, :], xnT[:, j, c0:c0 + cnt], xnT[:, j, 16:528],
                            start=(j == 0), stop=(j == 7)), waits=bw if j == 0 else [])
                    ev = op("vector", lambda e, bk=bank, k=k, cnt=cnt: e.tensor_tensor(
                        smt[0:cnt, k, :], bk[0:cnt, :], msk[0:cnt, k, :], op=ALU.mult),
                        waits=[("te", mk["te"]), ("ve", ve_msk)])
                    bank_free[id(bank)] = ("ve", ev["ve"])
                sm_done[b] = ("ve", P.val("ve"))
                bank = take_bank()
                bw = [tevac_done[b]]
                if bank_free[id(bank)]:
                    bw.append(bank_free[id(bank)])
                mk = None
                for j in range(8):
                    mk = op("tensor", lambda e, bk=bank, j=j: e.matmul(
                        bk[0:1, :], xnT[:, j, 15:16], xnT[:, j, 16:528],
                        start=(j == 0), stop=(j == 7)), waits=bw if j == 0 else [])
                srow_mm_done[b] = mk["te"]
                ev = op("scalar", lambda e, bk=bank: e.activation(
                    srow[0:1, :], bk[0:1, :], AF.Copy, scale=cols[0:1, L_G512I:L_G512I + 1]),
                    waits=[("te", mk["te"])])
                z = op("vector", lambda e: e.memset(srow[0:1, 0:1], 0.0), waits=[("sc", ev["sc"])])
                bank_free[id(bank)] = ("ve", z["ve"])
                srow_ev[b] = ("ve", z["ve"])

                s4_marks = [None] * 4
                def emit_quant(t):
                    r = t % 2
                    qw = [("sc", xh_done[t])]
                    if qst_free[r]:
                        qw.append(qst_free[r])
                    v1 = op("vector", lambda e, t=t, r=r: e.tensor_reduce(
                        qst[:, r, 0:1], xh16[:, t, :], AX.X, ALU.max,
                        apply_absolute_value=True), waits=qw)
                    s1 = op("scalar", lambda e, r=r: e.activation(
                        qst[:, r, 1:2], qst[:, r, 0:1], AF.Relu,
                        bias=fcs[:, C_NEPS:C_NEPS + 1]), waits=[("ve", v1["ve"])])
                    s2 = op("scalar", lambda e, r=r: e.activation(
                        qst[:, r, 2:3], qst[:, r, 1:2], AF.Identity,
                        scale=fcs[:, C_I7:C_I7 + 1], bias=fcs[:, C_EPS7:C_EPS7 + 1]))
                    v2 = op("vector", lambda e, r=r: e.reciprocal(qst[:, r, 3:4], qst[:, r, 2:3]),
                            waits=[("sc", s2["sc"])])
                    op("vector", lambda e, r=r: e.tensor_scalar(
                        qst[:, r, 4:5], qst[:, r, 2:3],
                        -8.0 if round_mode == "ladder" else -1536.0, None, op0=ALU.mult))
                    uw = [("ve", v2["ve"])]
                    if uq_free[0]:
                        uw.append(uq_free[0])
                    if round_mode == "ladder":
                        s3 = op("scalar", lambda e, t=t, r=r: e.activation(
                            u0q[:], xh16[:, t, :], AF.Identity, scale=qst[:, r, 3:4],
                            bias=fcs[:, C_85:C_85 + 1]), waits=uw)
                        op("vector", lambda e: e.tensor_copy(uq[:], u0q[:]),
                           waits=[("sc", s3["sc"])])
                        for bit in (8.0, 4.0, 2.0, 1.0):
                            op("vector", lambda e, bit=bit: e.tensor_scalar(
                                cq[:], uq[:], bit, None, op0=ALU.is_ge))
                            op("vector", lambda e, bit=bit: e.scalar_tensor_tensor(
                                uq[:], cq[:], -bit, uq[:], op0=ALU.mult, op1=ALU.add))
                        v4 = op("vector", lambda e: e.tensor_tensor(cq[:], u0q[:], uq[:],
                                                                    op=ALU.subtract))
                        xw = [("ve", v4["ve"])]
                        if xq_free[t]:
                            xw.append(xq_free[t])
                        s4 = op("scalar", lambda e, t=t, r=r: e.activation(
                            xqtm[:, t, :], cq[:], AF.Identity, scale=qst[:, r, 2:3],
                            bias=qst[:, r, 4:5]), waits=xw)
                        xh_free[t] = ("ve", v4["ve"])
                        qst_free[r] = ("ve", v4["ve"])
                    else:
                        s3 = op("scalar", lambda e, t=t, r=r: e.activation(
                            u0q[:], xh16[:, t, :], AF.Identity, scale=qst[:, r, 3:4],
                            bias=fcs[:, C_1536:C_1536 + 1]), waits=uw)
                        xw = [("sc", s3["sc"])]
                        if xq_free[t]:
                            xw.append(xq_free[t])
                        s4 = op("scalar", lambda e, t=t, r=r: e.activation(
                            xqtm[:, t, :], u0q[:], AF.Identity, scale=qst[:, r, 2:3],
                            bias=qst[:, r, 4:5]), waits=xw)
                        xh_free[t] = ("sc", s3["sc"])
                        qst_free[r] = ("sc", s4["sc"])
                    uq_free[0] = ("sc", s4["sc"])
                    s4_marks[t] = s4


                m7rot = [0]
                for t in range(4 if (parts & 4) else 0):
                    c0 = 16 + 128 * t
                    for h in range(2):
                        rk_ = m7rot[0] % 4
                        m7rot[0] += 1
                        if rk_ < 2:
                            bank_ap = pM[:, 512 * rk_:512 * (rk_ + 1)]
                            fkey = ("pm", rk_)
                        else:
                            bank_ap = (pA, pB)[rk_ - 2][:]
                            fkey = ("b", id((pA, pB)[rk_ - 2]))
                        bw = []
                        fm = pm_free[fkey[1]] if fkey[0] == "pm" else bank_free[fkey[1]]
                        if fm:
                            bw.append(fm)
                        if xh_free[t]:
                            bw.append(xh_free[t])
                        mk = None
                        for j in range(8):
                            mk = op("tensor", lambda e, bk=bank_ap, j=j, c0=c0, h=h: e.matmul(
                                bk, xnT[:, j, c0:c0 + 128], hT[:, j, 512 * h:512 * (h + 1)],
                                start=(j == 0), stop=(j == 7)), waits=bw if j == 0 else [])
                        ev = op("scalar", lambda e, bk=bank_ap, t=t, h=h: e.activation(
                            xh16[:, t, 512 * h:512 * (h + 1)], bk, AF.Copy),
                            waits=[("te", mk["te"])])
                        if fkey[0] == "pm":
                            pm_free[fkey[1]] = ("sc", ev["sc"])
                        else:
                            bank_free[fkey[1]] = ("sc", ev["sc"])
                    xh_done[t] = P.val("sc")

                def emit_txq(t):
                    s4 = s4_marks[t]

                    def evq(ba, bb, te_done, t=t):
                        op("scalar", lambda e, bk=ba: e.activation(
                            xqT[:, 0:8:2, 128 * t:128 * (t + 1)], bk, AF.Copy),
                            waits=[("te", te_done)])
                        e2 = op("scalar", lambda e, bk=bb: e.activation(
                            xqT[:, 1:8:2, 128 * t:128 * (t + 1)], bk, AF.Copy),
                            waits=[])
                        return ("sc", e2["sc"])

                    mkq = transpose_tile(
                        lambda j, t=t: xqtm[:, t, 128 * j:128 * (j + 1)], evq,
                        extra_waits=[("sc", s4["sc"])])
                    xq_free[t] = mkq
                    xqT_done[t] = mkq

                banks4 = [pA, pB, pC, pD]
                do_m5 = bool(parts & 2)
                skip_m3 = bool(parts & 16)
                skip_bnd = bool(parts & 32)
                njm5 = 1 if (parts & 64) else 8
                nhm5 = 1 if (parts & 128) else 2
                last_cast_te = None
                for h in range(nhm5 if do_m5 else 0):
                    cast_te_hist = {}
                    cast_mark = {}
                    for j in range(njm5):
                        if h == 0 and j <= 2 and j in early_cast:
                            cast_mark[j] = early_cast[j]
                            cast_last[j] = early_cast[j]
                            cwl = None
                        else:
                            cwl = []
                            if b > 0 and stupd_done[b - 1][j]:
                                cwl.append(("ve", stupd_done[b - 1][j]))
                            if j >= 3 and (j - 3) in cast_te_hist:
                                cwl.append(("te", cast_te_hist[j - 3]))
                            elif j <= 2 and h == 1 and last_cast_te:
                                cwl.append(("te", last_cast_te))
                        if cwl is None:
                            pass
                        elif j % 2 == 0:
                            cst = op("scalar", lambda e, j=j: e.activation(
                                st16[:, j % 3, :], st32[:, j, :], AF.Copy), waits=cwl)
                            cast_mark[j] = ("sc", cst["sc"])
                            cast_sc[j] = cst["sc"]
                        else:
                            cst = op("vector", lambda e, j=j: e.tensor_copy(
                                st16[:, j % 3, :], st32[:, j, :]), waits=cwl)
                            cast_mark[j] = ("ve", cst["ve"])
                            cast_sc[j] = None
                        mk = None
                        for ii in range(4):
                            i = 4 * h + ii
                            bank = banks4[ii]
                            bw = [cast_mark[j]]
                            if j == 0:
                                bw.append(tevac_done[b])
                                if bank_free[id(bank)]:
                                    bw.append(bank_free[id(bank)])
                            stp = (j == njm5 - 1) and skip_m3 and skip_bnd
                            mk = op("tensor", lambda e, bk=bank, j=j, i=i, stp=stp: e.matmul(
                                bk[:], st16[:, j % 3, 128 * i:128 * (i + 1)],
                                xnT[:, j, 16:528], start=(j == 0), stop=stp),
                                waits=bw)
                        cast_te_hist[j] = mk["te"]
                        cast_last[j] = cast_mark[j]
                    last_cast_te = cast_te_hist[7]
                    if not skip_m3:
                        for k in range(4):
                            cnt = 127 if k == 3 else 128
                            for ii in range(4):
                                i = 4 * h + ii
                                bank = banks4[ii]
                                bw = []
                                if k == 0 and ii == 0:
                                    bw = [sm_done[b], vgs_done[b]]
                                stp = (k == 3) and skip_bnd
                                op("tensor", lambda e, bk=bank, k=k, i=i, cnt=cnt, stp=stp: e.matmul(
                                    bk[:], vgs[0:cnt, k, 128 * i:128 * (i + 1)],
                                    smt[0:cnt, k, :], start=False, stop=stp), waits=bw)
                    for ii in range(4):
                        i = 4 * h + ii
                        bank = banks4[ii]
                        if not skip_bnd:
                            bw = [srow_ev[b], vgs_done[b]] if ii == 0 else []
                            mk = op("tensor", lambda e, bk=bank, i=i: e.matmul(
                                bk[:], vg0[0:1, 128 * i:128 * (i + 1)], srow[0:1, :],
                                start=False, stop=True), waits=bw)
                        tew = P.val("te")
                        ev = op("vector", lambda e, bk=bank, i=i: e.tensor_tensor(
                            rds[:, i, :], bk[:], gcb[:], op=ALU.mult),
                            waits=[("te", tew), ("ve", ve_gcb)])
                        bank_free[id(bank)] = ("ve", ev["ve"])
                    if (parts & 4) and h == 0:
                        emit_quant(0)
                    if (parts & 4) and h == 1:
                        emit_quant(1)
                    if h == 0 and mid_cb[0] is not None:
                        mid_cb[0]()
                        mid_cb[0] = None
                rd_done[b] = ("ve", P.val("ve")) if do_m5 else None
                xnT_rd_done[b] = ("te", P.val("te"))

            abanks = [(pM[:, 0:512], ("pm", 0)), (pM[:, 512:1024], ("pm", 1)),
                      (pA, ("b", id(pA))), (pB, ("b", id(pB))),
                      (pC, ("b", id(pC))), (pD, ("b", id(pD)))]
            bseq = [0, 1, 2, 3, 4, 5, 0, 1, 2, 3, 4, 5, 2, 3, 4, 5]
            aupd = [None] * 6
            gk = 0
            for e_ch in range(8):
                for h in range(2):
                    bank_ap, bkey = abanks[bseq[gk]]
                    bw = [vgs_done[b], wkrow_done[b]]
                    if aupd[bseq[gk]]:
                        bw.append(("ve", aupd[bseq[gk]]))
                    elif bkey[0] == "pm":
                        if pm_free[bkey[1]]:
                            bw.append(pm_free[bkey[1]])
                    else:
                        if bank_free[bkey[1]]:
                            bw.append(bank_free[bkey[1]])
                    ap = bank_ap if bkey[0] == "pm" else bank_ap[:]
                    mk = None
                    for k in range(4):
                        cnt = 127 if k == 3 else 128
                        mk = op("tensor", lambda e, k=k, e_ch=e_ch, h=h, cnt=cnt, b=b, ap=ap: e.matmul(
                            ap, xntm[0:cnt, (4 * b + k) % 6, 128 * e_ch:128 * (e_ch + 1)],
                            vgs[0:cnt, k, 512 * h:512 * (h + 1)],
                            start=(k == 0), stop=False), waits=bw if k == 0 else [])
                    mk = op("tensor", lambda e, e_ch=e_ch, h=h, ap=ap: e.matmul(
                        ap, wkr[0:1, 128 * e_ch:128 * (e_ch + 1)],
                        vg0[0:1, 512 * h:512 * (h + 1)], start=False, stop=True))
                    uw = [("te", mk["te"])]
                    if own and cast_last[e_ch]:
                        uw.append(cast_last[e_ch])
                    u = op("vector", lambda e, e_ch=e_ch, h=h, ap=ap: e.scalar_tensor_tensor(
                        st32[:, e_ch, 512 * h:512 * (h + 1)], st32[:, e_ch, 512 * h:512 * (h + 1)],
                        cols[:, L_GC:L_GC + 1], ap,
                        op0=ALU.mult, op1=ALU.add), waits=uw)
                    aupd[bseq[gk]] = u["ve"]
                    if bkey[0] == "pm":
                        pm_free[bkey[1]] = ("ve", u["ve"])
                    else:
                        bank_free[bkey[1]] = ("ve", u["ve"])
                    gk += 1
                stupd_done[b][e_ch] = P.val("ve")
                if own and (parts & 4) and e_ch in (1, 3, 5, 7):
                    emit_txq((e_ch - 1) // 2)
                if own and (parts & 4) and e_ch == 1:
                    emit_quant(2)
                if own and (parts & 4) and e_ch == 3:
                    emit_quant(3)
            m4_done[b] = P.val("te")
            if not own:
                for k in range(4):
                    tile_free.setdefault(4 * b + k, []).append(("te", m4_done[b]))

            if own:
                ob = b - OWN_B0
                for t in range(4 if (parts & 8) else 0):
                    g_own = 4 * ob + t
                    lw = []
                    if g_own >= 2 and cmb_store[g_own - 2] is not None:
                        lw.append(("qg", cmb_store[g_own - 2]))
                    dm = dma("gpsimd", "qg", xcb[:, t % 2, :],
                             x_d[WU + 512 * ob + 128 * t: WU + 512 * ob + 128 * t + 128, :],
                             waits=lw)
                    last_ev = None
                    for h in range(2):
                        bw = [rd_done[b], xqT_done[t]]
                        if pm_free[h]:
                            bw.append(pm_free[h])
                        if cmb_evac_free[h]:
                            bw.append(("ve", cmb_evac_free[h]))
                        mk = None
                        for i in range(8):
                            mk = op("tensor", lambda e, i=i, t=t, h=h: e.matmul(
                                pM[:, 512 * h:512 * (h + 1)], rds[:, i, 128 * t:128 * (t + 1)],
                                warT[:, i, 512 * h:512 * (h + 1)],
                                start=(i == 0), stop=False), waits=bw if i == 0 else [])
                        for j in range(8):
                            mk = op("tensor", lambda e, j=j, t=t, h=h: e.matmul(
                                pM[:, 512 * h:512 * (h + 1)], xqT[:, j, 128 * t:128 * (t + 1)],
                                wqT[:, j, 512 * h:512 * (h + 1)], start=False, stop=(j == 7)))
                        ev1 = op("vector", lambda e, t=t, h=h: e.tensor_tensor(
                            xcb[:, t % 2, 512 * h:512 * (h + 1)],
                            xcb[:, t % 2, 512 * h:512 * (h + 1)],
                            pM[:, 512 * h:512 * (h + 1)], op=ALU.add),
                            waits=[("te", mk["te"]), ("qg", dm["qg"])])
                        ev2 = op("vector", lambda e, t=t, h=h, b=b: e.scalar_tensor_tensor(
                            xcb[:, t % 2, 512 * h:512 * (h + 1)],
                            xntm[:, (4 * b + t) % 6, 512 * h:512 * (h + 1)], 1.0,
                            xcb[:, t % 2, 512 * h:512 * (h + 1)],
                            op0=ALU.mult, op1=ALU.add), waits=[("ve", ev1["ve"])])
                        cmb_evac_free[h] = ev2["ve"]
                        pm_free[h] = ("ve", ev1["ve"])
                        last_ev = ev2
                    tile_free.setdefault(4 * b + t, []).append(("ve", last_ev["ve"]))
                    st = dma("gpsimd", "qg",
                             out_d[512 * ob + 128 * t: 512 * ob + 128 * t + 128, :],
                             xcb[:, t % 2, :], waits=[("ve", last_ev["ve"])])
                    cmb_store[g_own] = st["qg"]

        if nbmax > 0:
            for g in range(4):
                prep_dma(g)
                prep_stats(g)
                prep_apply(g)
            trans_block(0)
        if do_weights or nbmax > 0:
            for r in range(8):
                wwT_mark[0] = weight_rowtile(wwT, r, ww_d[128 * r:128 * (r + 1), :])
        emit_gcb()
        for b in range(min(nbmax, NB)):
            if b == 1 and do_weights:
                for r in range(8):
                    weight_rowtile(wqT, r, wf_d[128 * r:128 * (r + 1), :], quant=True)
            if b == 2 and do_weights:
                for r in range(8):
                    weight_rowtile(warT, r, wr_d[128 * r:128 * (r + 1), :], scale_col=L_ACOL)
            if b + 1 < min(nbmax, NB):
                for g in range(4 * (b + 1), 4 * (b + 1) + 2):
                    prep_dma(g)
                    prep_stats(g)

                def _mid(bb=b + 1, cur=b):
                    prep_apply(4 * bb)
                    prep_apply(4 * bb + 1)
                    prep_dma(4 * bb + 2)
                    prep_dma(4 * bb + 3)
                    if cur < OWN_B0:
                        prep_stats(4 * bb + 2)
                        prep_stats(4 * bb + 3)
                mid_cb[0] = _mid

                def _cb(bb=b + 1, cur=b):
                    if cur >= OWN_B0:
                        prep_stats(4 * bb + 2)
                        prep_stats(4 * bb + 3)
                    prep_apply(4 * bb + 2)
                    prep_apply(4 * bb + 3)
                    trans_block(bb)
                trans_cb[0] = _cb
            body(b)
            if trans_cb[0] is not None:
                trans_cb[0]()
                trans_cb[0] = None
            if b + 1 < min(nbmax, NB):
                w = [("sc", ln_done[4 * b + 3]), ("te", m4_done[b])]
                dmw = dma("gpsimd", "qg", wkr[0:1, :], xntm[127:128, (4 * b + 3) % 6, :], waits=w)
                wkrow_done[b + 1] = ("qg", dmw["qg"])
                tile_free.setdefault(4 * b + 3, []).append(("qg", dmw["qg"]))

        if dbg_block is not None:
            dstg = xcb[:, 0, :]
            dbw = [("te", P.val("te")), ("sc", P.val("sc")), ("ve", P.val("ve")),
                   ("qg", P.val("qg"))]

            def dump16(dst, src_ap):
                w = dbw + ([("qs", P.val("qs"))] if False else [])
                n = src_ap.shape[-1]
                p = src_ap.shape[0]
                c = op("scalar", lambda e, d=src_ap, n=n, p=p: e.activation(dstg[0:p, 0:n], d, AF.Copy),
                       waits=dbw + [("qs", P.val("qs"))])
                dma("sync", "qs", dst, dstg[0:p, 0:n], waits=[("sc", c["sc"])])

            for t in range(4):
                dump16(dbg_vgs[t], vgs[:, t, :])
                dump16(dbg_smt[t], smt[:, t, :])
                dump16(dbg_msk[t], msk[:, t, :])
            for i in range(8):
                dump16(dbg_rds[i], rds[:, i, :])
                dump16(dbg_xnT[i], xnT[:, i, :])
                dump16(dbg_xqT[i], xqT[:, i, :])
                dma("sync", "qs", dbg_st[i], st32[:, i, :], waits=dbw)
            dma("sync", "qs", dbg_cols[:], cols[:], waits=dbw)
            dump16(dbg_gcb[:], gcb[:])
            dump16(dbg_rows[0:1], vg0[0:1, :])
            dump16(dbg_rows[1:2], wkr[0:1, :])
            dump16(dbg_rows[2:3, 0:512], srow[0:1, :])
            P.emit("sync", lambda e: e.wait_ge(sems["qs"], P.val("qs")), waits=(), incs=())
        P.emit("gpsimd", lambda e: e.wait_ge(sems["qg"], P.val("qg")), waits=(), incs=())

        globals()["_LAST_PROG"] = P
        with nc.Block() as block:
            selfsem = {"vector": "ve", "scalar": "sc"}

            def mkrun(name):
                def run(eng_obj):
                    own = selfsem.get(name)
                    prior = 0
                    for waits, fn, incs in P.ops[name]:
                        if own and prior:
                            eng_obj.wait_ge(sems[own], prior)
                        for s, v in waits:
                            eng_obj.wait_ge(sems[s], v)
                        inst = fn(eng_obj)
                        for s, a in incs:
                            inst.then_inc(sems[s], a)
                        if own:
                            prior += sum(a for s, a in incs if s == own)
                return run

            block.sync(mkrun("sync"))
            block.gpsimd(mkrun("gpsimd"))
            block.scalar(mkrun("scalar"))
            block.vector(mkrun("vector"))
            block.tensor(mkrun("tensor"))
    return nc


def _host_consts():
    bf = ml_dtypes.bfloat16
    ident = np.eye(128, dtype=bf)
    onesr = np.ones((1, 128), dtype=np.float32)
    onesr16 = np.ones((1, 128), dtype=bf)
    r = np.arange(128)
    bitc = np.stack([((r >> b) & 1).astype(np.float32) for b in range(7)]
                    + [r.astype(np.float32)], axis=1)
    bitc = np.ascontiguousarray(bitc)
    fc = np.zeros((128, 8), dtype=np.float32)
    fc[:, 0] = 1.0
    fc[:, 1] = -1e-5
    fc[:, 2] = 1e-5
    fc[:, 3] = 8.5
    fc[:, 4] = 1.0 / 7.0
    fc[:, 5] = 1e-5 / 7.0
    fc[:, 6] = 1536.0
    fc[:, 7] = -8.0
    kk = np.arange(512)[:, None]
    cc = np.arange(512)[None, :]
    cauS = (cc >= kk + 2).astype(bf)
    cauS = np.ascontiguousarray(cauS.reshape(4, 128, 512).transpose(1, 0, 2))
    h = np.array([[1.0]], dtype=np.float64)
    while h.shape[0] < D:
        h = np.block([[h, h], [h, -h]])
    had = np.ascontiguousarray((h / np.sqrt(D)).astype(np.float32).astype(bf))
    return ident, onesr, onesr16, bitc, fc, cauS, had


_NC_CACHE = [None]


def kernel(L0_attn_out, L0_attn_W_write, L0_attn_W_read, L0_attn_decay, L0_attn_log_alpha,
           L0_ffn_weight):
    x = np.ascontiguousarray(np.asarray(L0_attn_out, dtype=np.float32).reshape(T_FULL, D))
    ww = np.ascontiguousarray(np.asarray(L0_attn_W_write, dtype=np.float32))
    wr = np.ascontiguousarray(np.asarray(L0_attn_W_read, dtype=np.float32))
    wf = np.ascontiguousarray(np.asarray(L0_ffn_weight, dtype=np.float32))
    dsc = np.tile(np.array([[float(np.asarray(L0_attn_decay)),
                             float(np.asarray(L0_attn_log_alpha))]], dtype=np.float32), (128, 1))
    ident, onesr, onesr16, bitc, fc, cauS, had = _host_consts()

    if _NC_CACHE[0] is None:
        _NC_CACHE[0] = build_nc()
    nc = _NC_CACHE[0]

    xpad = np.concatenate([np.zeros((WU, D), dtype=np.float32), x], axis=0)
    in_maps = []
    for c in range(NCORES):
        s = c * TC
        in_maps.append({
            "x": np.ascontiguousarray(xpad[s: s + WU + TC]),
            "w_write": ww, "w_read": wr, "w_ffn": wf,
            "hadamard": had, "dscalars": dsc, "ident": ident,
            "onesrow": onesr, "onesrow16": onesr16, "bitcols": bitc, "fconsts": fc, "causalS": cauS,
        })
    res = run_bass_kernel_spmd(nc, in_maps, core_ids=list(range(NCORES)))
    out = np.concatenate([np.asarray(r["out"], dtype=np.float32) for r in res.results], axis=0)
    return out.reshape(1, T_FULL, D)
